# version 03
# baseline (speedup 1.0000x reference)
import math
import numpy as np
import ml_dtypes
import concourse.bass as bass
import concourse.mybir as mybir
from concourse.bass_utils import run_bass_kernel_spmd
from concourse.alu_op_type import AluOpType as OP

F32 = mybir.dt.float32
BF16 = mybir.dt.bfloat16
AF = mybir.ActivationFunctionType
AX = mybir.AxisListType.X

D = 1024
H = 32
NS = 64
PD = 64
DI = 2048
DP = 6432
DF = 4096
L = 128
SCALE = (NS * 2) ** -0.5
LNSCALE = math.log(SCALE)
PI = math.pi
OXR, OB = 2048, 6144
N_CORES = 8
ENGS = ("sync", "tensor", "vector", "scalar", "gpsimd")


def deps(*toks):
    w = {}
    for t in toks:
        if t:
            for k, v in t.items():
                w[k] = max(w.get(k, 0), v)
    return w


PARANOID = [True]


class Builder:
    def __init__(self, nc):
        self.nc = nc
        self.items = {e: [] for e in ENGS}
        self.counts = {}
        self.sems = {}
        self.close_list = []

    def sem(self, name):
        cm = self.nc.semaphore(name)
        self.sems[name] = cm.__enter__()
        self.close_list.append(cm)
        self.counts[name] = 0

    def st(self, eng, fn, waits=(), sem=None, n=1):
        s = sem if sem is not None else eng
        if PARANOID[0]:
            waits = dict(waits)
            for k, v in self.counts.items():
                waits[k] = max(waits.get(k, 0), v)
        self.counts[s] = self.counts.get(s, 0) + n
        self.items[eng].append((dict(waits), fn, s, n))
        return {s: self.counts[s]}

    def dma(self, eng, fn, waits=(), sem="dq"):
        return self.st(eng, fn, waits=waits, sem=sem, n=16)

    def tok(self, sem):
        return {sem: self.counts.get(sem, 0)}

    def sb(self, name, shape, dt=F32):
        cm = self.nc.sbuf_tensor(name, shape, dt)
        return cm, cm.__enter__()

    def ps(self, name, shape, dt=F32):
        cm = self.nc.psum_tensor(name, shape, dt)
        return cm, cm.__enter__()

    def free(self, *cms):
        for cm in cms:
            cm.__exit__(None, None, None)

    def emit(self):
        nc = self.nc
        sems = self.sems
        with nc.Block() as block:
            for eng in ENGS:
                def make(eng):
                    def body(e):
                        for waits, fn, s, n in self.items[eng]:
                            for k, v in waits.items():
                                e.wait_ge(sems[k], v)
                            inst = fn(e)
                            if inst is not None and s is not None:
                                inst.then_inc(sems[s], n)
                    return body
                body = make(eng)
                body.__name__ = "_"
                getattr(block, eng)(body)


def build_program(NT, dbg=False):
    assert NT % 4 == 0
    NCH = NT
    Tc = L * NT
    NQ = NT // 4
    nc = bass.Bass("TRN2", target_bir_lowering=False, detect_race_conditions=False)
    b = Builder(nc)
    for s in ("tensor", "vector", "scalar", "gpsimd", "dq", "gq", "cc", "oq"):
        b.sem(s)
    ST, DMA, TK = b.st, b.dma, b.tok

    def din(name, shape, dt=F32):
        return nc.dram_tensor(name, shape, dt, kind="ExternalInput").ap()

    def dint(name, shape, dt=F32):
        return nc.dram_tensor(name, shape, dt).ap()

    x_sl = din("x_sl", [Tc + 1, D])
    W_in = din("W_in", [D, DP])
    W_out = din("W_out", [DI, D])
    W1 = din("W1", [DF, D])
    W2 = din("W2", [D, DF])
    b1c = din("b1c", [128, 32])
    b2c = din("b2c", [128, 8])
    dtb_rep = din("dtb_rep", [128, H])
    alog_rep = din("alog_rep", [128, H])
    bbias_rep = din("bbias_rep", [128, 2 * NS])
    cbias_rep = din("cbias_rep", [128, NS])
    ident_bf = din("ident_bf", [128, 128], BF16)
    ident_f32 = din("ident_f32", [128, 128])
    ones_col = din("ones_col", [128, 1])
    onesb = din("onesb", [128, 128], BF16)
    maskS = din("maskS", [128, 4 * 128], BF16)
    maskP = din("maskP", [8, 1])
    maskIT = din("maskIT", [8, 8])
    out_d = nc.dram_tensor("out", [Tc, D], F32, kind="ExternalOutput").ap()

    xh_d = dint("xh_d", [Tc + 1, DF], BF16)
    yT_d = dint("yT_d", [DI, Tc], BF16)
    wq1t_d = dint("wq1t_d", [D, DF], BF16)
    wq2t_d = dint("wq2t_d", [DF, D], BF16)
    cumflat_d = dint("cumflat_d", [NCH, H * L])
    decflat_d = dint("decflat_d", [NCH, H * L], BF16)
    dec2flat_d = dint("dec2flat_d", [NCH, H * L], BF16)
    cdec_d = dint("cdec_d", [NCH, H])
    negO_d = dint("negO_d", [1, H])
    cose_d = dint("cose_d", [1, 2 * H])
    cf_d = dint("cf_d", [8, H])
    rs_d = dint("rs_d", [1, Tc])
    rq_d = dint("rq_d", [1, 2])
    cc1i = dint("cc1i", [1, 64])
    cc1o = dint("cc1o", [8, 64])
    x2_d = dint("x2_d", [Tc, D], BF16)
    cc2i = dint("cc2i", [NS, H * PD])
    cc2o = dint("cc2o", [8 * NS, H * PD])

    _, xnT = b.sb("xnT", [128, 8 * (NT + 1) * 128], BF16)
    _, lamdt_sb = b.sb("lamdt_sb", [128, NT * H], BF16)
    _, coeff_sb = b.sb("coeff_sb", [128, NT * H], BF16)
    _, CrT = b.sb("CrT", [64, Tc], BF16)
    _, ag1 = b.sb("ag1", [8, 64])
    _, consts = b.sb("consts", [128, 512])
    _, identb = b.sb("identb", [128, 128], BF16)
    _, onesb_s = b.sb("onesb_s", [128, 128], BF16)
    _, maskS_s = b.sb("maskS_s", [128, 512], BF16)
    _, mask8 = b.sb("mask8", [8, 9])
    _, stat = b.sb("stat", [128, 8])
    _, sm = b.sb("sm", [128, 128])
    _, negOc = b.sb("negOc", [32, 4])
    _, wT = b.sb("wT", [32, 128])
    _, red1 = b.sb("red1", [1, max(NT * H, 512)])
    _, cum_all = b.sb("cum_all", [32, Tc])
    _, CL_sb = b.sb("CL_sb", [32, NCH + 1])
    _, dec_hm = b.sb("dec_hm", [32, Tc], BF16)
    cm_Br, Br_sb = b.sb("Br_sb", [128, (NT + 1) * 128], BF16)
    cm_Bb, Bb_tm = b.sb("Bb_tm", [128, NT * 128], BF16)
    cm_BrT, BrT = b.sb("BrT", [64, 2 * (Tc + 1)], BF16)
    cm_ldA, ldA_sb = b.sb("ldA_sb", [128, NT * H])
    cm_cdR, cdecR = b.sb("cdecR", [64, NCH * H])
    cm_deT, deT_all = b.sb("deT_all", [128, NT * H], BF16)
    cm_cumT, cumT_all = b.sb("cumT_all", [128, NT * H])
    C_DTB, C_EXPA, C_BB, C_CB, C_B1, C_B2, C_ONE, C_IDF, C_COSE, C_RQ = (
        0, 32, 64, 192, 256, 288, 296, 300, 428, 492)

    cm_Bm, Bm_sb = b.sb("Bm_sb", [128, (NT + 1) * 128])
    cm_Cm, Cm_sb = b.sb("Cm_sb", [128, (NT + 1) * NS])
    cm_s3, s3_sb = b.sb("s3_sb", [128, NT * 96])
    cm_dt, dt_sb = b.sb("dt_sb", [128, NT * H])
    cm_w, w_sb = b.sb("w_sb", [128, NT * H])
    cm_ang, ang_sb = b.sb("ang_sb", [32, Tc])
    cm_cos, cos_tm = b.sb("cos_tm", [128, (NT + 1) * H])
    cm_sin, sin_tm = b.sb("sin_tm", [128, (NT + 1) * H])
    cm_xt, x_t = b.sb("x_t", [128, 1024])
    cm_ln, ln_t = b.sb("ln_t", [128, 1024])
    cm_xn, xn_t = b.sb("xn_t", [128, 1024], BF16)
    cm_wt, wtmp = b.sb("wtmp", [128, 1024])
    cm_xh, xh_s = b.sb("xh_s", [128, 1024], BF16)
    cm_ga, gacc = b.sb("gacc", [128, 66])
    cm_rp, rp = b.sb("rp", [128, 192])
    cm_Cr, Cr_sb = b.sb("Cr_sb", [128, (NT + 1) * NS], BF16)
    cm_W, W_sb = b.sb("W_sb", [128, 8 * 4384], BF16)

    cm_pA = []
    pA = []
    for i in range(4):
        cm, t = b.ps(f"pA{i}", [128, 512], F32)
        cm_pA.append(cm)
        pA.append(t)
    cm_pBa, pBa = b.ps("pB_bfa", [128, 1024], BF16)
    cm_pBb, pBb = b.ps("pB_bfb", [128, 1024], BF16)
    pBp = (pBa, pBb)
    cm_pCa, pCa = b.ps("pCa", [128, 512], F32)
    cm_pCb, pCb = b.ps("pCb", [128, 512], F32)
    pCp = (pCa, pCb)

    t_c = {}
    for dst, src in ((identb[:, :], ident_bf), (onesb_s[:, :], onesb),
                     (maskS_s[:, :], maskS), (mask8[:, 0:1], maskP),
                     (mask8[:, 1:9], maskIT),
                     (consts[:, C_DTB:C_DTB + 32], dtb_rep),
                     (consts[:, C_EXPA:C_EXPA + 32], alog_rep),
                     (consts[:, C_BB:C_BB + 128], bbias_rep),
                     (consts[:, C_CB:C_CB + 64], cbias_rep),
                     (consts[:, C_B1:C_B1 + 32], b1c),
                     (consts[:, C_B2:C_B2 + 8], b2c),
                     (consts[:, C_ONE:C_ONE + 1], ones_col),
                     (consts[:, C_IDF:C_IDF + 128], ident_f32)):
        t_c = DMA("sync", lambda e, dst=dst, src=src: e.dma_start(out=dst, in_=src))
    t_expa = ST("scalar", lambda e: e.activation(consts[:, C_EXPA:C_EXPA + 32],
                                                 consts[:, C_EXPA:C_EXPA + 32], AF.Exp),
                waits=deps(t_c))

    t_win = {}
    for fc in range(8):
        src = bass.AP(W_in.tensor, fc * 128 * DP + OXR, [[DP, 128], [1, 4384]])
        t_win = DMA("gpsimd", lambda e, fc=fc, src=src: e.dma_start(
            out=W_sb[:, fc * 4384:(fc + 1) * 4384], in_=src), sem="gq")

    NGRP = [(512 * i, 512) for i in range(8)] + [(4096, 288)]
    t_sm_t = [None] * (NT + 1)
    t_ev = {}
    t_mm = {}
    t_xd = {}
    t_tp_prev = {}
    t_te = {}
    t_te2 = {}
    for t in range(NT + 1):
        row0 = 1 + t * 128 if t < NT else 0
        t_x = DMA("sync", lambda e, row0=row0: e.dma_start(
            out=x_t[:, :], in_=bass.AP(x_sl.tensor, row0 * D, [[D, 128], [1, D]])),
            waits=deps(t_ev, TK("vector")))
        ST("vector", lambda e: e.tensor_reduce(stat[:, 0:1], x_t[:, :], axis=AX,
                                               op=OP.add), waits=deps(t_x))
        ST("vector", lambda e: e.tensor_scalar(stat[:, 1:2], stat[:, 0:1], 1.0 / D,
                                               None, op0=OP.mult))
        t_v = ST("vector", lambda e: e.tensor_scalar(ln_t[:, :], x_t[:, :],
                                                     stat[:, 1:2], None, op0=OP.subtract),
                 waits=deps(TK("vector")))
        if dbg and t == 0:
            dbg_x0 = nc.dram_tensor("dbg_x0", [128, 3 * 1024], F32, kind="ExternalOutput").ap()
            DMA("sync", lambda e: e.dma_start(
                out=bass.AP(dbg_x0.tensor, 0, [[3 * 1024, 128], [1, 1024]]),
                in_=x_t[:, :]), sem="oq", waits=deps(t_x, t_v))
            DMA("sync", lambda e: e.dma_start(
                out=bass.AP(dbg_x0.tensor, 1024, [[3 * 1024, 128], [1, 1024]]),
                in_=ln_t[:, :]), sem="oq", waits=deps(t_x, t_v))
        t_sq = ST("scalar", lambda e: e.activation(wtmp[:, :], ln_t[:, :], AF.Square),
                  waits=deps(t_v))
        if dbg and t == 0:
            t_dbq = DMA("sync", lambda e: e.dma_start(
                out=bass.AP(dbg_x0.tensor, 2048, [[3 * 1024, 128], [1, 1024]]),
                in_=wtmp[:, :]), sem="oq", waits=deps(t_sq))
        ST("vector", lambda e: e.tensor_reduce(stat[:, 2:3], wtmp[:, :], axis=AX,
                                               op=OP.add), waits=deps(t_sq))
        ST("vector", lambda e: e.tensor_scalar(stat[:, 3:4], stat[:, 2:3], 1.0 / D,
                                               1e-5, op0=OP.mult, op1=OP.add))
        t_rc = ST("vector", lambda e: e.reciprocal(stat[:, 4:5], stat[:, 3:4]))
        t_rs = ST("scalar", lambda e: e.activation(stat[:, 5:6], stat[:, 4:5], AF.Sqrt),
                  waits=deps(t_rc))
        t_xn = ST("vector", lambda e: e.tensor_scalar(xn_t[:, :], ln_t[:, :],
                                                      stat[:, 5:6], None, op0=OP.mult),
                  waits=deps(t_rs, t_tp_prev))
        for fc in range(8):
            bnk = pBp[fc % 2]
            t_tp = ST("tensor", lambda e, fc=fc, bnk=bnk: e.transpose(
                bnk[:, 0:128], xn_t[:, fc * 128:(fc + 1) * 128], identb[:, :]),
                waits=deps(t_xn, t_c, t_te2))
            t_te2 = t_te
            t_te = ST("vector", lambda e, fc=fc, t=t, bnk=bnk: e.tensor_copy(
                xnT[:, (fc * (NT + 1) + t) * 128:(fc * (NT + 1) + t) * 128 + 128],
                bnk[:, 0:128]), waits=deps(t_tp))
        t_tp_prev = t_tp
        t_xnT = deps(t_te, t_te2)
        for g, (c0, cw) in enumerate(NGRP):
            bank = pA[g % 3]
            for fc in range(8):
                t_mm = ST("tensor", lambda e, fc=fc, t=t, c0=c0, cw=cw, bank=bank: e.matmul(
                    bank[:, 0:cw],
                    xnT[:, (fc * (NT + 1) + t) * 128:(fc * (NT + 1) + t) * 128 + 128],
                    W_sb[:, fc * 4384 + c0:fc * 4384 + c0 + cw],
                    start=(fc == 0), stop=(fc == 7)),
                    waits=deps(t_xnT, t_win, t_ev) if fc == 0 else ())
            if g < 8:
                slot = (g % 2) * 512
                t_ev = ST("scalar", lambda e, g=g, bank=bank, slot=slot: e.activation(
                    xh_s[:, slot:slot + 512], bank[:, 0:512], AF.Silu),
                    waits=deps(t_mm, t_xd))
                if t < NT:
                    t_xd = DMA("sync", lambda e, t=t, g=g, slot=slot: e.dma_start(
                        out=bass.AP(xh_d.tensor, (1 + t * 128) * DF + g * 512,
                                    [[DF, 128], [1, 512]]),
                        in_=xh_s[:, slot:slot + 512]), waits=deps(t_ev))
                else:
                    t_xd = DMA("sync", lambda e, g=g, slot=slot: e.dma_start(
                        out=bass.AP(xh_d.tensor, g * 512, [[DF, 1], [1, 512]]),
                        in_=xh_s[0:1, slot:slot + 512]), waits=deps(t_ev))
            else:
                ST("scalar", lambda e, t=t, bank=bank: e.copy(
                    Bm_sb[:, t * 128:(t + 1) * 128], bank[:, 0:128]), waits=deps(t_mm))
                t_ev = ST("scalar", lambda e, t=t, bank=bank: e.copy(
                    Cm_sb[:, t * NS:(t + 1) * NS], bank[:, 128:192]))
                if t < NT:
                    t_ev = ST("scalar", lambda e, t=t, bank=bank: e.copy(
                        s3_sb[:, t * 96:(t + 1) * 96], bank[:, 192:288]))
                t_sm_t[t] = t_ev
    t_projmm = t_mm

    b.free(cm_W)
    cm_ch, cos_hm = b.sb("cos_hm", [32, Tc])
    cm_sh, sin_hm = b.sb("sin_hm", [32, Tc])
    cm_tr, trig = b.sb("trig", [32, Tc])
    cm_gt, gtmp = b.sb("gtmp", [128, 1024])
    cm_cdc, cdec_sb = b.sb("cdec_sb", [32, NCH])
    cm_wrA, wrA = b.sb("wrA", [32, Tc])
    cm_wrB, wrB = b.sb("wrB", [32, Tc])
    t_gz = ST("gpsimd", lambda e: e.memset(gacc[:, :], 0.0))
    gi = 0
    t_gred = t_gz
    for wsrc, ntile, wwidth in ((W1, 32, D), (W2, 8, DF)):
        for i in range(ntile):
            for piece in range(wwidth // 1024):
                src = bass.AP(wsrc.tensor, i * 128 * wwidth + piece * 1024,
                              [[wwidth, 128], [1, 1024]])
                t_gd = DMA("sync", lambda e, src=src: e.dma_start(out=gtmp[:, :], in_=src),
                           waits=deps(t_gred))
                t_gred = ST("vector", lambda e, gi=gi: e.tensor_reduce(
                    gacc[:, gi:gi + 1], gtmp[:, :], axis=AX, op=OP.add,
                    apply_absolute_value=True), waits=deps(t_gd, t_gz))
                gi += 1
    t_g1 = ST("vector", lambda e: e.tensor_reduce(gacc[:, 64:65], gacc[:, 0:32],
                                                  axis=AX, op=OP.add), waits=deps(t_gred))
    t_g2 = ST("vector", lambda e: e.tensor_reduce(gacc[:, 65:66], gacc[:, 32:64],
                                                  axis=AX, op=OP.add))
    t_gm = ST("tensor", lambda e: e.matmul(pA[2][0:1, 440:442], consts[:, C_ONE:C_ONE + 1],
                                           gacc[:, 64:66], start=True, stop=True),
              waits=deps(t_g1, t_g2, t_c))
    t_gs = ST("vector", lambda e: e.tensor_scalar(gacc[0:1, 64:66], pA[2][0:1, 440:442],
                                                  1.0 / (DF * D), 1e-5, op0=OP.mult,
                                                  op1=OP.add), waits=deps(t_gm))
    t_gr = ST("vector", lambda e: e.reciprocal(gacc[0:1, 64:66], gacc[0:1, 64:66]))
    t_gb = DMA("sync", lambda e: e.dma_start(out=rq_d, in_=gacc[0:1, 64:66]),
               waits=deps(t_gr))
    t_rq = DMA("gpsimd", lambda e: e.dma_start(
        out=consts[:, C_RQ:C_RQ + 2],
        in_=bass.AP(rq_d.tensor, 0, [[0, 128], [1, 2]])), sem="gq", waits=deps(t_gb))

    for t in range(NT):
        o = t * H
        s3o = t * 96
        ST("vector", lambda e, s3o=s3o: e.tensor_tensor(
            sm[:, 0:32], s3_sb[:, s3o:s3o + 32], consts[:, C_DTB:C_DTB + 32], op=OP.add),
            waits=deps(t_sm_t[t], t_expa))
        ST("scalar", lambda e: e.activation(sm[:, 32:64], sm[:, 0:32], AF.Abs),
           waits=deps(TK("vector")))
        t_sp0 = ST("scalar", lambda e: e.activation(sm[:, 64:96], sm[:, 32:64], AF.Exp,
                                                    scale=-1.0))
        t_sp1 = ST("vector", lambda e: e.tensor_scalar(sm[:, 96:128], sm[:, 64:96],
                                                       1.0, None, op0=OP.add),
                   waits=deps(t_sp0))
        t_sp2 = ST("scalar", lambda e: e.activation(sm[:, 32:64], sm[:, 96:128], AF.Ln),
                   waits=deps(t_sp1))
        t_sp3 = ST("vector", lambda e: e.tensor_scalar(sm[:, 96:128], sm[:, 0:32],
                                                       0.0, None, op0=OP.max),
                   waits=deps(t_sp2))
        t_sp = ST("vector", lambda e: e.tensor_tensor(sm[:, 32:64], sm[:, 32:64],
                                                      sm[:, 96:128], op=OP.add))
        ST("vector", lambda e, o=o: e.tensor_scalar(dt_sb[:, o:o + H], sm[:, 32:64],
                                                    1e-4, 0.5, op0=OP.max, op1=OP.min),
           waits=deps(t_sp))
        ST("vector", lambda e, o=o: e.tensor_tensor(
            sm[:, 0:32], dt_sb[:, o:o + H], consts[:, C_EXPA:C_EXPA + 32], op=OP.mult))
        ST("vector", lambda e, o=o: e.tensor_scalar(ldA_sb[:, o:o + H], sm[:, 0:32],
                                                    -1.0, None, op0=OP.mult))
        t_lm = ST("scalar", lambda e, s3o=s3o: e.activation(
            sm[:, 0:32], s3_sb[:, s3o + 32:s3o + 64], AF.Sigmoid),
            waits=deps(TK("vector")))
        ST("vector", lambda e, o=o: e.tensor_tensor(sm[:, 32:64], sm[:, 0:32],
                                                    dt_sb[:, o:o + H], op=OP.mult),
           waits=deps(t_lm))
        ST("vector", lambda e, o=o: e.tensor_copy(lamdt_sb[:, o:o + H], sm[:, 32:64]))
        ST("vector", lambda e, o=o: e.tensor_tensor(sm[:, 32:64], dt_sb[:, o:o + H],
                                                    sm[:, 32:64], op=OP.subtract))
        t_al = ST("scalar", lambda e, o=o: e.activation(sm[:, 0:32], ldA_sb[:, o:o + H],
                                                        AF.Exp), waits=deps(TK("vector")))
        ST("vector", lambda e: e.tensor_tensor(sm[:, 32:64], sm[:, 32:64], sm[:, 0:32],
                                               op=OP.mult), waits=deps(t_al))
        ST("vector", lambda e, o=o: e.tensor_copy(coeff_sb[:, o:o + H], sm[:, 32:64]))
        ST("vector", lambda e, o=o: e.tensor_reduce(sm[:, 0:1], dt_sb[:, o:o + H],
                                                    axis=AX, op=OP.add))
        ST("vector", lambda e: e.tensor_scalar(sm[:, 1:2], sm[:, 0:1], 1.0 / H, None,
                                               op0=OP.mult))
        t_w = ST("vector", lambda e, o=o, s3o=s3o: e.tensor_scalar(
            w_sb[:, o:o + H], s3_sb[:, s3o + 64:s3o + 96], sm[:, 1:2], None, op0=OP.mult),
            waits=deps(TK("vector")))
    t_wall = t_w

    t_scan = {}
    for t in range(NT):
        t_tw = ST("tensor", lambda e, t=t: e.transpose(
            pCa[0:32, 0:128], w_sb[:, t * H:(t + 1) * H], consts[:, C_IDF:C_IDF + 128]),
            waits=deps(t_wall, TK("vector")))
        ST("vector", lambda e: e.tensor_copy(wT[:, :], pCa[0:32, 0:128]),
           waits=deps(t_tw))
        ini = 0.0 if t == 0 else ang_sb[:, t * 128 - 1:t * 128]
        t_scan = ST("vector", lambda e, t=t, ini=ini: e.tensor_tensor_scan(
            ang_sb[:, t * 128:(t + 1) * 128], wT[:, :], wT[:, :], ini,
            op0=OP.add, op1=OP.bypass), waits=deps(TK("vector")))
        t_tl = ST("tensor", lambda e, t=t: e.transpose(
            pCb[0:32, 0:128], ldA_sb[:, t * H:(t + 1) * H],
            consts[:, C_IDF:C_IDF + 128]), waits=deps(t_scan))
        ST("vector", lambda e: e.tensor_copy(wT[:, :], pCb[0:32, 0:128]),
           waits=deps(t_tl))
        t_cum = ST("vector", lambda e, t=t: e.tensor_tensor_scan(
            cum_all[:, t * 128:(t + 1) * 128], wT[:, :], wT[:, :], 0.0,
            op0=OP.add, op1=OP.bypass))
        t_tc = ST("tensor", lambda e, t=t: e.transpose(
            pCa[:, 128:160], cum_all[:, t * 128:(t + 1) * 128],
            consts[0:32, C_IDF:C_IDF + 32]), waits=deps(t_cum))
        ST("vector", lambda e, t=t: e.tensor_copy(
            cumT_all[:, t * H:(t + 1) * H], pCa[:, 128:160]), waits=deps(t_tc))
        t_de = ST("scalar", lambda e, t=t: e.activation(
            wT[:, :], cum_all[:, t * 128:(t + 1) * 128], AF.Exp,
            bias=cum_all[:, t * 128 + 127:t * 128 + 128], scale=-1.0),
            waits=deps(TK("vector")))
        t_dt = ST("tensor", lambda e: e.transpose(
            pCb[:, 128:160], wT[:, :], consts[0:32, C_IDF:C_IDF + 32]), waits=deps(t_de))
        t_scan = ST("vector", lambda e, t=t: e.tensor_copy(
            deT_all[:, t * H:(t + 1) * H], pCb[:, 128:160]), waits=deps(t_dt))
    t_cumall = t_scan

    cum_last = cum_all[:, :].rearrange("p (c l) -> p c l", l=128)[:, :, 127:128].opt()
    t_cd = ST("scalar", lambda e: e.activation(cdec_sb[:, :], cum_last, AF.Exp),
              waits=deps(t_cumall))
    ST("vector", lambda e: e.memset(CL_sb[:, 0:1], 0.0), waits=deps(t_cumall))
    t_cl = ST("vector", lambda e: e.tensor_tensor_scan(
        CL_sb[:, 1:NCH + 1], cum_last, cum_last, 0.0, op0=OP.add, op1=OP.bypass))
    t_cdb = DMA("sync", lambda e: e.dma_start(
        out=bass.AP(cdec_d.tensor, 0, [[1, 32], [32, NCH]]), in_=cdec_sb[:, :],
        allow_slow_non_contiguous=True), waits=deps(t_cd))
    t_cdr = DMA("gpsimd", lambda e: e.dma_start(
        out=cdecR[:, :], in_=bass.AP(cdec_d.tensor, 0, [[0, 64], [1, NCH * 32]])),
        sem="gq", waits=deps(t_cdb))
    t_cfb = DMA("sync", lambda e: e.dma_start(
        out=bass.AP(cumflat_d.tensor, 0, [[128, 32], [H * L, NCH], [1, 128]]),
        in_=cum_all[:, :].rearrange("p (c l) -> p c l", l=128)), waits=deps(t_cumall))
    t_lns = ST("vector", lambda e: e.memset(negOc[0:32, 3:4], LNSCALE),
               waits=deps(t_cumall))
    t_ds = ST("scalar", lambda e: e.activation(dec_hm[:, :], cum_all[:, :], AF.Exp,
                                               bias=negOc[0:32, 3:4]),
              waits=deps(t_cumall, t_lns))
    t_dsb = DMA("sync", lambda e: e.dma_start(
        out=bass.AP(decflat_d.tensor, 0, [[128, 32], [H * L, NCH], [1, 128]]),
        in_=dec_hm[:, :].rearrange("p (c l) -> p c l", l=128)), waits=deps(t_ds))

    t_m1 = ST("tensor", lambda e: e.matmul(pA[2][0:1, 0:NT * H],
                                           consts[:, C_ONE:C_ONE + 1],
                                           w_sb[:, :NT * H], start=True, stop=True),
              waits=deps(t_wall, t_projmm, TK("vector")))
    t_e1 = ST("vector", lambda e: e.tensor_copy(red1[:, 0:NT * H],
                                                pA[2][0:1, 0:NT * H]), waits=deps(t_m1))
    t_sk = ST("vector", lambda e: e.tensor_reduce(
        ag1[0:1, 0:32], red1[0:1, 0:NT * H].rearrange("p (t i) -> p i t", i=32),
        axis=AX, op=OP.add), waits=deps(t_e1))
    t_m2 = ST("tensor", lambda e: e.matmul(pA[2][0:1, 0:NT * H],
                                           consts[:, C_ONE:C_ONE + 1],
                                           ldA_sb[:, :NT * H], start=True, stop=True),
              waits=deps(t_sk))
    t_e2 = ST("vector", lambda e: e.tensor_copy(red1[:, 0:NT * H],
                                                pA[2][0:1, 0:NT * H]), waits=deps(t_m2))
    t_ld = ST("vector", lambda e: e.tensor_reduce(
        ag1[0:1, 32:64], red1[0:1, 0:NT * H].rearrange("p (t i) -> p i t", i=32),
        axis=AX, op=OP.add), waits=deps(t_e2))
    t_d1 = DMA("sync", lambda e: e.dma_start(out=cc1i, in_=ag1[0:1, 0:64]),
               waits=deps(t_sk, t_ld))
    t_cc1 = ST("gpsimd", lambda e: e.collective_compute(
        "AllGather", OP.bypass, replica_groups=[list(range(N_CORES))],
        ins=[cc1i.opt()], outs=[cc1o.opt()]), waits=deps(t_d1), sem="cc")
    t_ag1 = DMA("sync", lambda e: e.dma_start(out=ag1[:, :], in_=cc1o),
                waits=deps(t_cc1))

    if dbg:
        dbg_xnT = nc.dram_tensor("dbg_xnT", [128, 8 * (NT + 1) * 128], BF16, kind="ExternalOutput").ap()
        DMA("sync", lambda e: e.dma_start(out=dbg_xnT, in_=xnT[:, :]), sem="oq",
            waits=deps(t_projmm, TK("vector")))
        dbg_s3 = nc.dram_tensor("dbg_s3", [128, NT * 96], F32, kind="ExternalOutput").ap()
        DMA("sync", lambda e: e.dma_start(out=dbg_s3, in_=s3_sb[:, :]), sem="oq",
            waits=deps(t_sm_t[NT - 1], t_sm_t[NT]))
        dbg_ag1 = nc.dram_tensor("dbg_ag1", [8, 64], F32, kind="ExternalOutput").ap()
        DMA("sync", lambda e: e.dma_start(out=dbg_ag1, in_=ag1[:, :]), sem="oq",
            waits=deps(t_ag1))
    t_ok = ST("tensor", lambda e: e.matmul(pA[2][0:1, 0:32], mask8[:, 0:1],
                                           ag1[:, 0:32], start=True, stop=True),
              waits=deps(t_ag1))
    t_no = ST("vector", lambda e: e.tensor_scalar(red1[:, 0:32], pA[2][0:1, 0:32],
                                                  -1.0, None, op0=OP.mult),
              waits=deps(t_ok))
    t_nb = DMA("sync", lambda e: e.dma_start(out=negO_d, in_=red1[0:1, 0:32]),
               waits=deps(t_no))
    t_nc = DMA("sync", lambda e: e.dma_start(
        out=negOc[:, 0:1], in_=bass.AP(negO_d.tensor, 0, [[1, 32], [1, 1]])),
        waits=deps(t_nb))


    def wrap_ladder(tgt, sa, sb, wv):
        t2 = wv
        for S in (128 * PI, 64 * PI, 32 * PI, 16 * PI, 8 * PI, 4 * PI, 2 * PI):
            ST("vector", lambda e, S=S, tgt=tgt, sa=sa: e.tensor_scalar(
                sa, tgt, S / 2, None, op0=OP.is_gt), waits=t2)
            ST("vector", lambda e, S=S, tgt=tgt, sb=sb: e.tensor_scalar(
                sb, tgt, -S / 2, None, op0=OP.is_lt))
            ST("vector", lambda e, sa=sa, sb=sb: e.tensor_tensor(sa, sa, sb,
                                                                 op=OP.subtract))
            ST("vector", lambda e, S=S, sa=sa: e.tensor_scalar(sa, sa, S, None,
                                                               op0=OP.mult))
            t2 = ST("vector", lambda e, tgt=tgt, sa=sa: e.tensor_tensor(
                tgt, tgt, sa, op=OP.subtract))
            t2 = deps(t2)
        return t2

    def trig_chain(shift, dst, wprev):
        ST("vector", lambda e: e.tensor_scalar(trig[:, :], ang_sb[:, :], -1.0, None,
                                               op0=OP.mult),
           waits=deps(t_cumall, t_nc, wprev))
        ST("vector", lambda e, shift=shift: e.tensor_scalar(
            trig[:, :], trig[:, :], negOc[:, 0:1], shift, op0=OP.add, op1=OP.add))
        t2 = wrap_ladder(trig[:, :], wrA[:, :], wrB[:, :], ())
        return ST("scalar", lambda e, dst=dst: e.activation(dst[:, :], trig[:, :],
                                                            AF.Sin), waits=deps(t2))

    t_sinh = trig_chain(0.0, sin_hm, {})
    t_cosh = trig_chain(PI / 2, cos_hm, t_sinh)
    t_x1 = ST("vector", lambda e: e.tensor_scalar(negOc[:, 1:2], negOc[:, 0:1], 0.0,
                                                  None, op0=OP.add), waits=deps(t_cosh))
    t_x1 = wrap_ladder(negOc[:, 1:2], wrA[:, 0:1], wrB[:, 0:1], deps(t_x1))
    t_x2b = ST("vector", lambda e: e.tensor_scalar(negOc[:, 2:3], negOc[:, 0:1],
                                                   PI / 2, None, op0=OP.add))
    t_x3 = wrap_ladder(negOc[:, 2:3], wrA[:, 0:1], wrB[:, 0:1], deps(t_x2b))
    t_se = ST("scalar", lambda e: e.activation(negOc[:, 1:2], negOc[:, 1:2], AF.Sin),
              waits=deps(t_x1, t_x3))
    t_ce = ST("scalar", lambda e: e.activation(negOc[:, 2:3], negOc[:, 2:3], AF.Sin))
    t_eb = DMA("sync", lambda e: e.dma_start(
        out=bass.AP(cose_d.tensor, 0, [[1, 32], [1, 1]]), in_=negOc[:, 2:3]),
        waits=deps(t_ce, t_se))
    t_eb = DMA("sync", lambda e: e.dma_start(
        out=bass.AP(cose_d.tensor, 32, [[1, 32], [1, 1]]), in_=negOc[:, 1:2]))
    t_er = DMA("gpsimd", lambda e: e.dma_start(
        out=consts[:, C_COSE:C_COSE + 64],
        in_=bass.AP(cose_d.tensor, 0, [[0, 128], [1, 64]])), sem="gq", waits=deps(t_eb))

    t_cs = {}
    for t in range(NT):
        t_t1 = ST("tensor", lambda e, t=t: e.transpose(
            pCa[:, 256:288], cos_hm[:, t * 128:(t + 1) * 128],
            consts[0:32, C_IDF:C_IDF + 32]), waits=deps(t_cosh, t_cs))
        t_cs = ST("vector", lambda e, t=t: e.tensor_copy(cos_tm[:, t * H:(t + 1) * H],
                                                         pCa[:, 256:288]),
                  waits=deps(t_t1))
        t_t2 = ST("tensor", lambda e, t=t: e.transpose(
            pCb[:, 256:288], sin_hm[:, t * 128:(t + 1) * 128],
            consts[0:32, C_IDF:C_IDF + 32]), waits=deps(t_cs))
        t_cs = ST("vector", lambda e, t=t: e.tensor_copy(sin_tm[:, t * H:(t + 1) * H],
                                                         pCb[:, 256:288]),
                  waits=deps(t_t2))
    t_cs = ST("vector", lambda e: e.tensor_copy(cos_tm[:, NT * H:(NT + 1) * H],
                                                consts[:, C_COSE:C_COSE + 32]),
              waits=deps(t_er, t_cs))
    t_cs = ST("vector", lambda e: e.tensor_copy(sin_tm[:, NT * H:(NT + 1) * H],
                                                consts[:, C_COSE + 32:C_COSE + 64]))

    t_rope = {}
    for t in range(NT + 1):
        ob = t * 128
        oc = t * NS
        t_q = ST("scalar", lambda e, ob=ob: e.activation(
            rp[:, 0:128], Bm_sb[:, ob:ob + 128], AF.Square),
            waits=deps(t_sm_t[t], t_cs, t_rope))
        ST("vector", lambda e: e.reduce_sum(stat[:, 0:1], rp[:, 0:64], axis=AX),
           waits=deps(t_q))
        ST("vector", lambda e: e.reduce_sum(stat[:, 1:2], rp[:, 64:128], axis=AX))
        t_q2 = ST("scalar", lambda e, oc=oc: e.activation(
            rp[:, 128:192], Cm_sb[:, oc:oc + NS], AF.Square), waits=deps(TK("vector")))
        ST("vector", lambda e: e.reduce_sum(stat[:, 2:3], rp[:, 128:192], axis=AX),
           waits=deps(t_q2))
        ST("vector", lambda e: e.tensor_scalar(stat[:, 0:3], stat[:, 0:3], 1.0 / NS,
                                               1e-6, op0=OP.mult, op1=OP.add))
        t_r1 = ST("vector", lambda e: e.reciprocal(stat[:, 0:3], stat[:, 0:3]))
        t_r2 = ST("scalar", lambda e: e.activation(stat[:, 4:7], stat[:, 0:3], AF.Sqrt),
                  waits=deps(t_r1))
        ST("vector", lambda e, ob=ob: e.tensor_scalar(rp[:, 0:64], Bm_sb[:, ob:ob + 64],
                                                      stat[:, 4:5], None, op0=OP.mult),
           waits=deps(t_r2))
        ST("vector", lambda e, ob=ob: e.tensor_scalar(rp[:, 64:128],
                                                      Bm_sb[:, ob + 64:ob + 128],
                                                      stat[:, 5:6], None, op0=OP.mult))
        ST("vector", lambda e: e.tensor_tensor(rp[:, 0:128], rp[:, 0:128],
                                               consts[:, C_BB:C_BB + 128], op=OP.mult))
        ST("vector", lambda e, oc=oc: e.tensor_scalar(
            rp[:, 128:192], Cm_sb[:, oc:oc + NS], stat[:, 6:7], None, op0=OP.mult))
        ST("vector", lambda e: e.tensor_tensor(rp[:, 128:192], rp[:, 128:192],
                                               consts[:, C_CB:C_CB + 64], op=OP.mult))
        co = cos_tm[:, t * H:(t + 1) * H]
        si = sin_tm[:, t * H:(t + 1) * H]
        for base in (0, 64, 128):
            ST("vector", lambda e, base=base, co=co: e.tensor_tensor(
                wtmp[:, base:base + 32], rp[:, base:base + 32], co, op=OP.mult))
            ST("vector", lambda e, base=base, si=si: e.tensor_tensor(
                wtmp[:, 192 + base:224 + base], rp[:, base + 32:base + 64], si,
                op=OP.mult))
            ST("vector", lambda e, base=base: e.tensor_tensor(
                wtmp[:, base:base + 32], wtmp[:, base:base + 32],
                wtmp[:, 192 + base:224 + base], op=OP.subtract))
            ST("vector", lambda e, base=base, si=si: e.tensor_tensor(
                wtmp[:, 224 + base:256 + base], rp[:, base:base + 32], si, op=OP.mult))
            ST("vector", lambda e, base=base, co=co: e.tensor_tensor(
                wtmp[:, 192 + base:224 + base], rp[:, base + 32:base + 64], co,
                op=OP.mult))
            ST("vector", lambda e, base=base: e.tensor_tensor(
                wtmp[:, base + 32:base + 64], wtmp[:, 224 + base:256 + base],
                wtmp[:, 192 + base:224 + base], op=OP.add))
        ST("vector", lambda e, ob=ob: e.tensor_copy(Br_sb[:, ob:ob + 128],
                                                    wtmp[:, 0:128]))
        t_rope = ST("vector", lambda e, oc=oc: e.tensor_copy(Cr_sb[:, oc:oc + NS],
                                                             wtmp[:, 128:192]))
    t_ropeall = t_rope

    if dbg:
        dbg_cos = nc.dram_tensor("dbg_cos", [128, (NT + 1) * H], F32, kind="ExternalOutput").ap()
        dbg_sin = nc.dram_tensor("dbg_sin", [128, (NT + 1) * H], F32, kind="ExternalOutput").ap()
        dbg_Br = nc.dram_tensor("dbg_Br", [128, (NT + 1) * 128], BF16, kind="ExternalOutput").ap()
        dbg_Cr = nc.dram_tensor("dbg_Cr", [128, (NT + 1) * NS], BF16, kind="ExternalOutput").ap()
        dbg_cum = nc.dram_tensor("dbg_cum", [32, Tc], F32, kind="ExternalOutput").ap()
        dbg_smalls = nc.dram_tensor("dbg_smalls", [128, NT * H * 4], F32, kind="ExternalOutput").ap()
        DMA("sync", lambda e: e.dma_start(out=dbg_cos, in_=cos_tm[:, :]), sem="oq", waits=deps(t_ropeall))
        DMA("sync", lambda e: e.dma_start(out=dbg_sin, in_=sin_tm[:, :]), sem="oq", waits=deps(t_ropeall))
        DMA("sync", lambda e: e.dma_start(out=dbg_Br, in_=Br_sb[:, :]), sem="oq", waits=deps(t_ropeall))
        DMA("sync", lambda e: e.dma_start(out=dbg_Cr, in_=Cr_sb[:, :]), sem="oq", waits=deps(t_ropeall))
        DMA("sync", lambda e: e.dma_start(out=dbg_cum, in_=cum_all[:, :]), sem="oq", waits=deps(t_ropeall))
        DMA("sync", lambda e: e.dma_start(
            out=bass.AP(dbg_smalls.tensor, 0, [[NT * H * 4, 128], [1, NT * H]]),
            in_=dt_sb[:, :]), sem="oq", waits=deps(t_ropeall))
        DMA("sync", lambda e: e.dma_start(
            out=bass.AP(dbg_smalls.tensor, NT * H, [[NT * H * 4, 128], [1, NT * H]]),
            in_=ldA_sb[:, :]), sem="oq", waits=deps(t_ropeall))
        DMA("gpsimd", lambda e: e.dma_start(
            out=bass.AP(dbg_smalls.tensor, 2 * NT * H, [[NT * H * 4, 128], [1, NT * H]]),
            in_=lamdt_sb[:, :]), sem="gq", waits=deps(t_ropeall))
        DMA("gpsimd", lambda e: e.dma_start(
            out=bass.AP(dbg_smalls.tensor, 3 * NT * H, [[NT * H * 4, 128], [1, NT * H]]),
            in_=coeff_sb[:, :]), sem="gq", waits=deps(t_ropeall))
    t_bt = {}
    t_bt2 = {}
    ib = [0]
    def alt():
        ib[0] ^= 1
        return pBp[ib[0]]
    for t in range(NT):
        for r in range(2):
            bnk = alt()
            t_t = ST("tensor", lambda e, t=t, r=r, bnk=bnk: e.transpose(
                bnk[0:64, 0:128], Br_sb[:, t * 128 + r * 64:t * 128 + r * 64 + 64],
                identb[:, :]), waits=deps(t_ropeall, t_bt2))
            t_bt2 = t_bt
            t_bt = ST("vector", lambda e, t=t, r=r, bnk=bnk: e.tensor_copy(
                BrT[:, r * (Tc + 1) + t * 128 + 1:r * (Tc + 1) + t * 128 + 129],
                bnk[0:64, 0:128]), waits=deps(t_t))
        bnk = alt()
        t_t = ST("tensor", lambda e, t=t, bnk=bnk: e.transpose(
            bnk[0:64, 0:128], Cr_sb[:, t * NS:(t + 1) * NS], identb[:, :]),
            waits=deps(t_bt2))
        t_bt2 = t_bt
        t_bt = ST("vector", lambda e, t=t, bnk=bnk: e.tensor_copy(
            CrT[:, t * 128:(t + 1) * 128], bnk[0:64, 0:128]), waits=deps(t_t))
    for r in range(2):
        bnk = alt()
        t_t = ST("tensor", lambda e, r=r, bnk=bnk: e.transpose(
            bnk[0:64, 0:1], Br_sb[0:1, NT * 128 + r * 64:NT * 128 + r * 64 + 64],
            identb[0:1, 0:1]), waits=deps(t_bt2))
        t_bt2 = t_bt
        t_bt = ST("vector", lambda e, r=r, bnk=bnk: e.tensor_copy(
            BrT[:, r * (Tc + 1):r * (Tc + 1) + 1], bnk[0:64, 0:1]), waits=deps(t_t))
    for c in range(NCH):
        for r in range(2):
            bnk = alt()
            t_t = ST("tensor", lambda e, c=c, r=r, bnk=bnk: e.transpose(
                bnk[:, 512:576],
                BrT[:, r * (Tc + 1) + c * 128:r * (Tc + 1) + c * 128 + 128],
                identb[0:64, 0:64]), waits=deps(t_bt2))
            t_bt2 = t_bt
            t_bt = ST("vector", lambda e, c=c, r=r, bnk=bnk: e.tensor_copy(
                Bb_tm[:, c * 128 + r * 64:c * 128 + r * 64 + 64], bnk[:, 512:576]),
                waits=deps(t_t))
    t_btall = deps(t_bt, t_bt2)

    b.free(cm_pCb, cm_pCa, cm_pBb, cm_pBa)
    b.free(cm_wrB, cm_wrA, cm_cdc, cm_gt, cm_tr, cm_sh, cm_ch)
    b.free(cm_W2del) if False else None
    b.free(cm_Cr, cm_rp, cm_ga, cm_xh, cm_wt, cm_xn, cm_ln, cm_xt,
           cm_sin, cm_cos, cm_ang, cm_w, cm_dt, cm_s3, cm_Cm, cm_Bm)
    cm_pQ = []
    pQ = []
    for i in range(4):
        cm, tns = b.ps(f"pQ{i}", [64, 512], F32)
        cm_pQ.append(cm)
        pQ.append(tns)
    cm_Mp, Mp_sb = b.sb("Mp_sb", [64, H * L], BF16)
    cm_D64, DEC64 = b.sb("DEC64", [64, H * L], BF16)
    _, Sin_bf = b.sb("Sin_bf", [64, DI], BF16)
    cm_Sc, Scur = b.sb("Scur", [64, DI])
    cm_Sp, Spv = b.sb("Spv", [64, DI], BF16)
    cm_xhg, xhg = b.sb("xhg", [128, DF], BF16)
    cm_xhb, xhb = b.sb("xhb", [128, DF], BF16)
    cm_Xg, Xg = b.sb("Xg", [128, 2 * DI], BF16)
    cm_Xb, Xb = b.sb("Xb", [128, 2 * DI], BF16)
    cm_RB, RB = b.sb("RB", [128, H * L])
    cm_E, E_sb = b.sb("E_sb", [128, H * L], BF16)
    cm_s, s_sb = b.sb("s_sb", [128, 512], BF16)
    cm_M, M_sb = b.sb("M_sb", [128, H * 4 * L], BF16)
    cm_ye, yev = b.sb("yev", [64, 512], BF16)
    dif = M_sb[:, 0:H * L * 2].bitcast(F32)

    t_z1 = ST("gpsimd", lambda e: e.memset(Scur[:, :], 0.0), waits=deps(t_btall))
    t_z2 = ST("gpsimd", lambda e: e.memset(Spv[:, :], 0.0))
    t_scan = deps(t_z1, t_z2)
    t_ydma = {}
    t_ycons = {}
    t_stmm = {}
    t_dfp = {}
    t_Mpp = {}
    t_Ep = {}
    for c in range(NCH):
        t_lg = DMA("sync", lambda e, c=c: e.dma_start(
            out=xhg[:, :], in_=bass.AP(xh_d.tensor, (1 + c * 128) * DF,
                                       [[DF, 128], [1, DF]])),
            waits=deps(t_ycons, t_stmm))
        t_lb = DMA("sync", lambda e, c=c: e.dma_start(
            out=xhb[:, :], in_=bass.AP(xh_d.tensor, c * 128 * DF,
                                       [[DF, 128], [1, DF]])))
        t_rb = DMA("gpsimd", lambda e, c=c: e.dma_start(
            out=RB[:, :],
            in_=bass.AP(cumflat_d.tensor, c * H * L, [[0, 128], [1, H * L]])),
            sem="gq", waits=deps(t_cfb, t_dfp))
        t_dc = DMA("gpsimd", lambda e, c=c: e.dma_start(
            out=DEC64[:, :],
            in_=bass.AP(decflat_d.tensor, c * H * L, [[0, 64], [1, H * L]])),
            sem="gq", waits=deps(t_dsb, t_Mpp))
        t_X = {}
        for r in range(2):
            xv = xhg[:, :].rearrange("p (h r q) -> p h r q", h=H, r=2)[:, :, r, :]
            lv = lamdt_sb[:, c * H:(c + 1) * H].unsqueeze(2).broadcast_to([128, H, PD])
            t_X = ST("vector", lambda e, r=r, xv=xv, lv=lv: e.tensor_tensor(
                Xg[:, r * DI:(r + 1) * DI].rearrange("p (h q) -> p h q", h=H),
                xv, lv, op=OP.mult), waits=deps(t_lg, t_ycons, t_stmm))
            xv2 = xhb[:, :].rearrange("p (h r q) -> p h r q", h=H, r=2)[:, :, r, :]
            cv = coeff_sb[:, c * H:(c + 1) * H].unsqueeze(2).broadcast_to([128, H, PD])
            t_X = ST("vector", lambda e, r=r, xv2=xv2, cv=cv: e.tensor_tensor(
                Xb[:, r * DI:(r + 1) * DI].rearrange("p (h q) -> p h q", h=H),
                xv2, cv, op=OP.mult), waits=deps(t_lb))
        t_sc = {}
        for j, (src, r) in enumerate(((1, 0), (1, 1), (0, 0), (0, 1))):
            lhs = BrT[:, r * (Tc + 1) + c * 128 + src:r * (Tc + 1) + c * 128 + src + 128]
            rhs = CrT[:, c * 128:(c + 1) * 128]
            t_sc = ST("tensor", lambda e, j=j, lhs=lhs, rhs=rhs: e.matmul(
                pA[0][:, j * 128:(j + 1) * 128], lhs, rhs, start=True, stop=True),
                waits=deps(t_btall, t_ycons) if j == 0 else ())
        t_smk = ST("vector", lambda e: e.tensor_tensor(s_sb[:, :], pA[0][:, :],
                                                       maskS_s[:, :], op=OP.mult),
                   waits=deps(t_sc))
        cv2 = cumT_all[:, c * H:(c + 1) * H].unsqueeze(2).broadcast_to([128, H, L])
        t_df = ST("vector", lambda e, cv2=cv2: e.scalar_tensor_tensor(
            dif.rearrange("p (h l) -> p h l", h=H),
            RB[:, :].rearrange("p (h l) -> p h l", h=H), 0.0, cv2,
            op0=OP.add, op1=OP.subtract), waits=deps(t_rb, t_Ep, t_ycons))
        t_dfp = t_df
        t_mn = ST("vector", lambda e: e.tensor_scalar(dif, dif, 0.0, None,
                                                      op0=OP.min))
        t_E = ST("scalar", lambda e: e.activation(E_sb[:, :], dif, AF.Exp),
                 waits=deps(t_mn, t_ycons))
        t_Ep = t_E
        sv = s_sb[:, :].rearrange("p (s l) -> p s l", s=4).unsqueeze(1).broadcast_to([128, H, 4, L])
        ev = E_sb[:, :].rearrange("p (h l) -> p h l", h=H).unsqueeze(2).broadcast_to([128, H, 4, L])
        t_M = ST("vector", lambda e, sv=sv, ev=ev: e.tensor_tensor(
            M_sb[:, :].rearrange("p (h s l) -> p h s l", h=H, s=4), sv, ev,
            op=OP.mult), waits=deps(t_smk, t_E, t_ycons))
        cb = CrT[:, c * 128:(c + 1) * 128].unsqueeze(1).broadcast_to([64, H, L])
        t_Mp = ST("vector", lambda e, cb=cb: e.tensor_tensor(
            Mp_sb[:, :].rearrange("p (h l) -> p h l", h=H), cb,
            DEC64[:, :].rearrange("p (h l) -> p h l", h=H), op=OP.mult),
            waits=deps(t_dc, t_ycons))
        t_Mpp = t_Mp
        t_spv = ST("vector", lambda e: e.tensor_copy(Spv[:, :], Scur[:, :]),
                   waits=deps(t_scan, t_ycons))
        t_ymm = {}
        t_wave = {}
        for wv in range(2):
            for hh in range(16):
                h = wv * 16 + hh
                bank = pA[hh // 4]
                col = (hh % 4) * 128
                for j, (XT, r) in enumerate(((Xg, 0), (Xg, 1), (Xb, 0), (Xb, 1))):
                    lhs = XT[:, r * DI + h * PD:r * DI + (h + 1) * PD]
                    rhs = M_sb[:, (h * 4 + j) * 128:(h * 4 + j + 1) * 128]
                    w0 = deps(t_M, t_X, t_spv, t_Mp, t_wave) if (hh == 0 and j == 0) else ()
                    t_ymm = ST("tensor", lambda e, bank=bank, col=col, lhs=lhs, rhs=rhs,
                               j=j: e.matmul(bank[0:64, col:col + 128], lhs, rhs,
                                             start=(j == 0), stop=False), waits=w0)
                lhs5 = Spv[:, h * PD:(h + 1) * PD]
                rhs5 = Mp_sb[:, h * 128:(h + 1) * 128]
                t_ymm = ST("tensor", lambda e, bank=bank, col=col, lhs5=lhs5, rhs5=rhs5:
                           e.matmul(bank[0:64, col:col + 128], lhs5, rhs5,
                                    start=False, stop=True))
            for bk in range(4):
                h0 = wv * 16 + bk * 4
                t_ye = ST("vector", lambda e, bk=bk: e.tensor_copy(
                    yev[:, :], pA[bk][0:64, :]), waits=deps(t_ymm, t_ydma))
                t_ydma = DMA("sync", lambda e, h0=h0, c=c: e.dma_start(
                    out=bass.AP(yT_d.tensor, h0 * PD * Tc + c * 128,
                                [[Tc, 64], [PD * Tc, 4], [1, 128]]),
                    in_=yev[:, :].rearrange("p (h l) -> p h l", h=4)),
                    waits=deps(t_ye))
            t_wave = deps(t_ydma)
        t_ycons = deps(t_ymm, t_ydma)
        t_X2 = {}
        for XT in (Xg, Xb):
            dv = deT_all[:, c * H:(c + 1) * H].unsqueeze(2).broadcast_to([128, H, PD])
            for r in range(2):
                t_X2 = ST("gpsimd", lambda e, XT=XT, r=r, dv=dv: e.tensor_tensor(
                    XT[:, r * DI:(r + 1) * DI].rearrange("p (h q) -> p h q", h=H),
                    XT[:, r * DI:(r + 1) * DI].rearrange("p (h q) -> p h q", h=H),
                    dv, op=OP.mult), waits=deps(t_ymm, t_z2) if XT is Xg and r == 0 else ())
        t_st = {}
        for j in range(4):
            r = j % 2
            XT = Xg if j < 2 else Xb
            if j < 2:
                lhs = Br_sb[:, c * 128 + r * 64:c * 128 + r * 64 + 64]
            else:
                lhs = Bb_tm[:, c * 128 + r * 64:c * 128 + r * 64 + 64]
            for nt4 in range(4):
                rhs = XT[:, r * DI + nt4 * 512:r * DI + (nt4 + 1) * 512]
                w0 = deps(t_X2, t_scan) if (j == 0 and nt4 == 0) else ()
                t_st = ST("tensor", lambda e, lhs=lhs, rhs=rhs, nt4=nt4, j=j: e.matmul(
                    pQ[nt4][:, :], lhs, rhs, start=(j == 0), stop=(j == 3)), waits=w0)
        t_stmm = t_st
        cdv = cdecR[:, c * H:(c + 1) * H].unsqueeze(2).broadcast_to([64, H, PD])
        t_s1 = ST("gpsimd", lambda e, cdv=cdv: e.tensor_tensor(
            Scur[:, :].rearrange("p (h q) -> p h q", h=H),
            Scur[:, :].rearrange("p (h q) -> p h q", h=H), cdv, op=OP.mult),
            waits=deps(t_spv, t_cdr, t_X2))
        t_sca = {}
        for nt4 in range(4):
            t_sca = ST("vector", lambda e, nt4=nt4: e.tensor_tensor(
                Scur[:, nt4 * 512:(nt4 + 1) * 512], Scur[:, nt4 * 512:(nt4 + 1) * 512],
                pQ[nt4][:, :], op=OP.add), waits=deps(t_s1, t_st))
        t_scan = t_sca

    t_sd = DMA("sync", lambda e: e.dma_start(out=cc2i, in_=Scur[:, :]),
               waits=deps(t_scan))
    t_cc2 = ST("gpsimd", lambda e: e.collective_compute(
        "AllGather", OP.bypass, replica_groups=[list(range(N_CORES))],
        ins=[cc2i.opt()], outs=[cc2o.opt()]), waits=deps(t_sd), sem="cc")

    b.free(cm_ye, cm_M, cm_s, cm_E, cm_RB, cm_Xb, cm_Xg, cm_xhb, cm_xhg)
    for cm in reversed(cm_pQ):
        b.free(cm)
    cm_pB2a, pB2a = b.ps("pB2a", [128, 1024], BF16)
    cm_pB2b, pB2b = b.ps("pB2b", [128, 1024], BF16)
    pB2p = (pB2a, pB2b)
    cm_qw, qw = b.sb("qw", [128, 1024])
    cm_qv, qv = b.sb("qv", [128, 1024])
    cm_qa, qa = b.sb("qa", [128, 1024])
    cm_qt, qt = b.sb("qt", [128, 1024], BF16)
    cm_qtT, qtT = b.sb("qtT", [128, 1024], BF16)
    t_qdma = {}
    t_qred = {}
    for widx, (wsrc, wdst, ntile, wwidth, ddim) in enumerate(
            ((W1, wq1t_d, 32, D, DF), (W2, wq2t_d, 8, DF, D))):
        rqc = consts[:, C_RQ + widx:C_RQ + widx + 1]
        for i in range(ntile):
            for piece in range(wwidth // 1024):
                src = bass.AP(wsrc.tensor, i * 128 * wwidth + piece * 1024,
                              [[wwidth, 128], [1, 1024]])
                t_wl = DMA("sync", lambda e, src=src: e.dma_start(out=qw[:, :], in_=src),
                           waits=deps(t_qred, t_rq, TK("tensor"), TK("vector"),
                                      TK("scalar")) if not t_qred else deps(t_qred, t_rq))
                ST("vector", lambda e, rqc=rqc: e.tensor_scalar(
                    qv[:, :], qw[:, :], rqc, None, op0=OP.mult), waits=deps(t_wl))
                ST("vector", lambda e: e.tensor_scalar(qa[:, :], qv[:, :], 0.5, None,
                                                       op0=OP.is_gt))
                ST("vector", lambda e: e.tensor_scalar(qv[:, :], qv[:, :], -0.5, None,
                                                       op0=OP.is_lt))
                t_qq = ST("vector", lambda e: e.tensor_tensor(qt[:, :], qa[:, :],
                                                              qv[:, :], op=OP.subtract))
                t_qred = t_qq
                t_qe = {}
                t_qe2 = {}
                for dc in range(8):
                    bnk = pB2p[dc % 2]
                    t_qt2 = ST("tensor", lambda e, dc=dc, bnk=bnk: e.transpose(
                        bnk[:, 0:128], qt[:, dc * 128:(dc + 1) * 128],
                        identb[:, :]), waits=deps(t_qq, t_qe2))
                    t_qe2 = t_qe
                    t_qe = ST("vector", lambda e, dc=dc, bnk=bnk: e.tensor_copy(
                        qtT[:, dc * 128:(dc + 1) * 128], bnk[:, 0:128]),
                        waits=deps(t_qt2))
                t_qe = deps(t_qe, t_qe2)
                dst = bass.AP(wdst.tensor, (piece * 1024) * ddim + i * 128,
                              [[ddim, 128], [128 * ddim, 8], [1, 128]])
                t_qdma = DMA("sync", lambda e, dst=dst: e.dma_start(
                    out=dst, in_=qtT[:, :].rearrange("p (dc q) -> p dc q", dc=8)),
                    waits=deps(t_qe))

    cm_cfR, cfR = b.sb("cfR", [64, 8 * H])
    cm_Sj, Sj = b.sb("Sj", [64, DI])
    cm_St2, St2 = b.sb("St2", [64, DI])
    cm_Sin, Sin = b.sb("Sin", [64, DI])
    t_w8 = ST("tensor", lambda e: e.matmul(pA[0][0:8, 0:32], mask8[:, 1:9],
                                           ag1[:, 32:64], start=True, stop=True),
              waits=deps(t_ag1, t_ycons, t_qdma))
    t_cf = ST("scalar", lambda e: e.activation(wT[0:8, 0:32], pA[0][0:8, 0:32], AF.Exp),
              waits=deps(t_w8))
    t_cf = ST("vector", lambda e: e.tensor_scalar(wT[0:8, 0:32], wT[0:8, 0:32],
                                                  mask8[:, 0:1], None, op0=OP.mult),
              waits=deps(t_cf, t_qred))
    t_cfd = DMA("sync", lambda e: e.dma_start(out=cf_d, in_=wT[0:8, 0:32]),
                waits=deps(t_cf))
    t_cfr = DMA("gpsimd", lambda e: e.dma_start(
        out=cfR[:, :], in_=bass.AP(cf_d.tensor, 0, [[0, 64], [1, 8 * H]])), sem="gq",
        waits=deps(t_cfd))
    t_si = ST("gpsimd", lambda e: e.memset(Sin[:, :], 0.0))
    t_sa = {}
    for j in range(8):
        t_sl = DMA("sync", lambda e, j=j: e.dma_start(
            out=Sj[:, :], in_=bass.AP(cc2o.tensor, j * NS * H * PD,
                                      [[H * PD, NS], [1, H * PD]])),
            waits=deps(t_cc2, t_sa))
        jv = cfR[:, j * H:(j + 1) * H].unsqueeze(2).broadcast_to([64, H, PD])
        t_sm2 = ST("vector", lambda e, jv=jv: e.tensor_tensor(
            St2[:, :].rearrange("p (h q) -> p h q", h=H),
            Sj[:, :].rearrange("p (h q) -> p h q", h=H), jv, op=OP.mult),
            waits=deps(t_sl, t_cfr, t_si))
        t_sa = ST("vector", lambda e: e.tensor_tensor(Sin[:, :], Sin[:, :], St2[:, :],
                                                      op=OP.add))
    t_sbf = ST("vector", lambda e: e.tensor_copy(Sin_bf[:, :], Sin[:, :]),
               waits=deps(t_sa))
    t_d2 = {}
    for c in range(NCH):
        t_bc = ST("vector", lambda e, c=c: e.tensor_scalar(
            negOc[:, 3:4], CL_sb[:, c:c + 1], 1.0, LNSCALE, op0=OP.mult, op1=OP.add),
            waits=deps(t_cl, t_sbf, t_d2))
        t_dh = ST("scalar", lambda e, c=c: e.activation(
            dec_hm[:, c * 128:(c + 1) * 128], cum_all[:, c * 128:(c + 1) * 128],
            AF.Exp, bias=negOc[:, 3:4]), waits=deps(t_bc))
        t_d2 = DMA("sync", lambda e, c=c: e.dma_start(
            out=bass.AP(dec2flat_d.tensor, c * H * L, [[128, 32], [1, 128]]),
            in_=dec_hm[:, c * 128:(c + 1) * 128]), waits=deps(t_dh))

    b.free(cm_Sin, cm_St2, cm_Sj, cm_cfR, cm_qtT, cm_qt, cm_qa, cm_qv, cm_qw)
    b.free(cm_Sp, cm_Sc)
    _, x2t = b.sb("x2t", [128, 1024], BF16)
    _, rsc = b.sb("rsc", [128, 4])
    cm_x3, x_t3 = b.sb("x_t3", [128, 1024])
    cm_g2, gtmp2 = b.sb("gtmp2", [128, 512])
    cm_Wz, Wz = b.sb("Wz", [128, 8 * DI], BF16)
    cm_Wo, Wo = b.sb("Wo", [128, 2 * 16 * 128], BF16)
    cm_zs, zs = b.sb("zs", [128, 16 * 512], BF16)
    cm_gs, gsb = b.sb("gsb", [128, 16 * 512], BF16)
    cm_ys, ysb = b.sb("ysb", [128, DI], BF16)
    cm_yes, yesb = b.sb("yesb", [128, DI], BF16)
    cm_PT, PT = b.sb("PT", [128, 8 * 512], BF16)
    t_wz = {}
    for fc in range(8):
        src = bass.AP(W_in.tensor, fc * 128 * DP, [[DP, 128], [1, DI]])
        t_wz = DMA("gpsimd", lambda e, fc=fc, src=src: e.dma_start(
            out=Wz[:, fc * DI:(fc + 1) * DI], in_=src), sem="gq",
            waits=deps(t_cfr, t_sbf, t_d2))
    t_wo = {}

    t_x2done = {}
    t_prev = {}
    t_tev = {}
    t_tev2 = {}
    for tg in range(NQ):
        t_zev = {}
        for fcz in range(16):
            bank = pA[fcz % 2]
            t_zmm = {}
            for dc in range(8):
                w0 = deps(t_wz, t_zev, t_w8, t_prev, t_sbf, TK("scalar")) if dc == 0 else ()
                t_zmm = ST("tensor", lambda e, fcz=fcz, dc=dc, tg=tg, bank=bank: e.matmul(
                    bank[:, :], Wz[:, dc * DI + fcz * 128:dc * DI + fcz * 128 + 128],
                    xnT[:, (dc * (NT + 1) + tg * 4) * 128:(dc * (NT + 1) + tg * 4) * 128 + 512],
                    start=(dc == 0), stop=(dc == 7)), waits=w0)
            t_zev = ST("scalar", lambda e, fcz=fcz, bank=bank: e.activation(
                zs[:, fcz * 512:(fcz + 1) * 512], bank[:, :], AF.Silu),
                waits=deps(t_zmm))
        t_ge = {}
        for ci in range(4):
            c = tg * 4 + ci
            t_dcl = DMA("gpsimd", lambda e, c=c: e.dma_start(
                out=DEC64[:, :],
                in_=bass.AP(dec2flat_d.tensor, c * H * L, [[0, 64], [1, H * L]])),
                sem="gq", waits=deps(t_d2, t_ge, t_Mpp, t_wo))
            cb = CrT[:, c * 128:(c + 1) * 128].unsqueeze(1).broadcast_to([64, H, L])
            t_mpp = ST("vector", lambda e, cb=cb: e.tensor_tensor(
                Mp_sb[:, :].rearrange("p (h l) -> p h l", h=H), cb,
                DEC64[:, :].rearrange("p (h l) -> p h l", h=H), op=OP.mult),
                waits=deps(t_dcl, t_sbf))
            t_Mpp = t_mpp
            t_yl = DMA("sync", lambda e, c=c: e.dma_start(
                out=ysb[:, :], in_=bass.AP(yT_d.tensor, c * 128,
                                           [[Tc, 128], [128 * Tc, 16], [1, 128]])),
                waits=deps(t_ge, t_sbf))
            t_cm = {}
            for h in range(H):
                bank = pA[h // 8]
                row = (h % 2) * 64
                col = ((h // 2) % 4) * 128
                tp = (0, 64) if row else None
                lhs = Sin_bf[:, h * PD:(h + 1) * PD]
                rhs = Mp_sb[:, h * 128:(h + 1) * 128]
                w0 = deps(t_mpp, t_zev, t_ge) if h == 0 else ()
                t_cm = ST("tensor", lambda e, bank=bank, row=row, col=col, tp=tp,
                          lhs=lhs, rhs=rhs: e.matmul(
                    bank[row:row + 64, col:col + 128], lhs, rhs, start=True, stop=True,
                    tile_position=tp), waits=w0)
            t_yee = {}
            for bk in range(4):
                yv = ysb[:, :].rearrange("p (g l) -> p g l", g=16)[:, bk * 4:(bk + 1) * 4, :]
                t_yee = ST("vector", lambda e, bk=bk, yv=yv: e.tensor_tensor(
                    yesb[:, bk * 512:(bk + 1) * 512].rearrange("p (g l) -> p g l", g=4),
                    pA[bk][:, :].rearrange("p (g l) -> p g l", g=4), yv, op=OP.add),
                    waits=deps(t_cm, t_yl))
            for fc in range(16):
                t_ge = ST("vector", lambda e, fc=fc, ci=ci: e.tensor_tensor(
                    gsb[:, fc * 512 + ci * 128:fc * 512 + ci * 128 + 128],
                    yesb[:, fc * 128:(fc + 1) * 128],
                    zs[:, fc * 512 + ci * 128:fc * 512 + ci * 128 + 128], op=OP.mult),
                    waits=deps(t_yee, t_zev))
        t_sqm = {}
        t_sq = {}
        for fc in range(16):
            t_sq = ST("scalar", lambda e, fc=fc: e.activation(
                gtmp2[:, 0:512], gsb[:, fc * 512:(fc + 1) * 512],
                AF.Square), waits=deps(t_ge, t_sqm))
            t_sqm = ST("tensor", lambda e, fc=fc: e.matmul(
                pA[0][0:1, :], consts[:, C_ONE:C_ONE + 1], gtmp2[:, 0:512],
                start=(fc == 0), stop=(fc == 15)), waits=deps(t_sq))
        t_rr = ST("vector", lambda e: e.tensor_scalar(
            red1[0:1, 0:512], pA[0][0:1, :], 1.0 / DI, 1e-6, op0=OP.mult, op1=OP.add),
            waits=deps(t_sqm))
        t_rr = ST("vector", lambda e: e.reciprocal(red1[0:1, 0:512], red1[0:1, 0:512]))
        t_rr = ST("scalar", lambda e: e.activation(red1[0:1, 0:512], red1[0:1, 0:512],
                                                   AF.Sqrt), waits=deps(t_rr))
        t_rd = DMA("sync", lambda e, tg=tg: e.dma_start(
            out=bass.AP(rs_d.tensor, tg * 512, [[512, 1], [1, 512]]),
            in_=red1[0:1, 0:512]), waits=deps(t_rr))
        t_pt = {}
        for do in range(8):
            wslot = (do % 2) * 2048
            wsrc = bass.AP(W_out.tensor, do * 128, [[D, 128], [128 * D, 16], [1, 128]])
            t_wo = DMA("gpsimd", lambda e, wslot=wslot, wsrc=wsrc: e.dma_start(
                out=Wo[:, wslot:wslot + 2048].rearrange("p (di q) -> p di q", di=16),
                in_=wsrc), sem="gq", waits=deps(t_wo, TK("tensor")))
            bank = pA[1 + (do % 2)]
            t_wm = {}
            for di in range(16):
                w0 = deps(t_ge, t_wo, t_pt, t_rr) if di == 0 else ()
                t_wm = ST("tensor", lambda e, di=di, bank=bank, wslot=wslot: e.matmul(
                    bank[:, :], Wo[:, wslot + di * 128:wslot + di * 128 + 128],
                    gsb[:, di * 512:(di + 1) * 512], start=(di == 0), stop=(di == 15)),
                    waits=w0)
            t_pt = ST("vector", lambda e, do=do, bank=bank: e.tensor_copy(
                PT[:, do * 512:(do + 1) * 512], bank[:, :]), waits=deps(t_wm))
        for ti in range(4):
            t = tg * 4 + ti
            t_rl = DMA("sync", lambda e, t=t: e.dma_start(
                out=rsc[:, 0:1], in_=bass.AP(rs_d.tensor, t * 128, [[1, 128], [1, 1]])),
                waits=deps(t_rd))
            t_xr = DMA("sync", lambda e, t=t: e.dma_start(
                out=x_t3[:, :], in_=bass.AP(x_sl.tensor, (1 + t * 128) * D,
                                            [[D, 128], [1, D]])),
                waits=deps(t_x2done, t_sbf))
            for do in range(8):
                bnk = pB2p[do % 2]
                t_tp = ST("tensor", lambda e, do=do, ti=ti, bnk=bnk: e.transpose(
                    bnk[:, 0:128],
                    PT[:, do * 512 + ti * 128:do * 512 + ti * 128 + 128], identb[:, :]),
                    waits=deps(t_pt, t_tev2))
                t_tev2 = t_tev
                t_tev = ST("vector", lambda e, do=do, bnk=bnk: e.scalar_tensor_tensor(
                    x2t[:, do * 128:do * 128 + 128],
                    bnk[:, 0:128], rsc[:, 0:1],
                    x_t3[:, do * 128:(do + 1) * 128], op0=OP.mult, op1=OP.add),
                    waits=deps(t_tp, t_rl, t_xr))
            t_x2done = DMA("sync", lambda e, t=t: e.dma_start(
                out=bass.AP(x2_d.tensor, t * 128 * D, [[D, 128], [1, D]]),
                in_=x2t[:, :]), waits=deps(t_tev, t_tev2))
        t_prev = deps(t_ge, t_rr, t_pt, t_x2done)

    b.free(cm_PT, cm_yes, cm_ys, cm_gs, cm_zs, cm_Wo, cm_Wz, cm_g2)
    _, xn2T = b.sb("xn2T", [128, 8 * Tc], BF16)
    cm_l1, lt2 = b.sb("lt2", [128, 1024])
    cm_l2, lw2 = b.sb("lw2", [128, 1024])
    cm_l3, ln2n = b.sb("ln2n", [128, 1024], BF16)
    cm_l4, x2l = b.sb("x2l", [128, 1024], BF16)
    t_l2 = {}
    t_te = {}
    t_te2 = {}
    for t in range(NT):
        t_xl2 = DMA("sync", lambda e, t=t: e.dma_start(
            out=x2l[:, :], in_=bass.AP(x2_d.tensor, t * 128 * D, [[D, 128], [1, D]])),
            waits=deps(t_x2done, t_l2))
        ST("vector", lambda e: e.tensor_reduce(
            stat[:, 0:1], x2l[:, :], axis=AX, op=OP.add), waits=deps(t_xl2))
        ST("vector", lambda e: e.tensor_scalar(stat[:, 1:2], stat[:, 0:1], 1.0 / D,
                                               None, op0=OP.mult))
        t_v = ST("vector", lambda e: e.tensor_scalar(
            lt2[:, :], x2l[:, :], stat[:, 1:2], None, op0=OP.subtract),
            waits=deps(TK("vector")))
        t_s = ST("scalar", lambda e: e.activation(lw2[:, :], lt2[:, :], AF.Square),
                 waits=deps(t_v))
        ST("vector", lambda e: e.tensor_reduce(stat[:, 2:3], lw2[:, :], axis=AX,
                                               op=OP.add), waits=deps(t_s))
        ST("vector", lambda e: e.tensor_scalar(stat[:, 3:4], stat[:, 2:3], 1.0 / D,
                                               1e-5, op0=OP.mult, op1=OP.add))
        t_rc = ST("vector", lambda e: e.reciprocal(stat[:, 4:5], stat[:, 3:4]))
        t_rs2 = ST("scalar", lambda e: e.activation(stat[:, 5:6], stat[:, 4:5], AF.Sqrt),
                   waits=deps(t_rc))
        t_xn2 = ST("vector", lambda e: e.tensor_scalar(ln2n[:, :], lt2[:, :],
                                                       stat[:, 5:6], None, op0=OP.mult),
                   waits=deps(t_rs2, t_tp_prev))
        for dc in range(8):
            bnk = pB2p[dc % 2]
            t_tp = ST("tensor", lambda e, dc=dc, bnk=bnk: e.transpose(
                bnk[:, 0:128], ln2n[:, dc * 128:(dc + 1) * 128], identb[:, :]),
                waits=deps(t_xn2, t_te2, t_tev))
            t_tp_prev = t_tp
            t_te2 = t_te
            t_te = ST("vector", lambda e, dc=dc, t=t, bnk=bnk: e.tensor_copy(
                xn2T[:, dc * Tc + t * 128:dc * Tc + t * 128 + 128],
                bnk[:, 0:128]), waits=deps(t_tp))
        t_l2 = deps(t_te, t_te2)

    b.free(cm_l4, cm_l3, cm_l2, cm_l1)
    _, wqb = b.sb("wqb", [128, 1024], BF16)
    _, hT = b.sb("hT", [128, 32 * 512], BF16)
    _, fT = b.sb("fT", [128, 8 * 512], BF16)
    _, outt = b.sb("outt", [128, 1024])
    t_h = {}
    t_fe = {}
    t_od = {}
    t_oe = {}
    t_oe2 = {}
    for q in range(NQ):
        for fco in range(32):
            t_wl = DMA("sync", lambda e, fco=fco: e.dma_start(
                out=wqb[:, :], in_=bass.AP(wq1t_d.tensor, fco * 128,
                                           [[DF, 128], [128 * DF, 8], [1, 128]])),
                waits=deps(t_qdma, t_h, t_l2))
            bank = pA[fco % 2]
            t_fm = {}
            for dc in range(8):
                w0 = deps(t_wl, t_l2, t_h, t_prev, t_fe) if dc == 0 else ()
                t_fm = ST("tensor", lambda e, dc=dc, q=q, bank=bank: e.matmul(
                    bank[:, :], wqb[:, dc * 128:(dc + 1) * 128],
                    xn2T[:, dc * Tc + q * 512:dc * Tc + q * 512 + 512],
                    start=(dc == 0), stop=(dc == 7)), waits=w0)
            t_h = ST("vector", lambda e, fco=fco, bank=bank: e.tensor_scalar(
                hT[:, fco * 512:(fco + 1) * 512], bank[:, :],
                consts[:, C_B1 + fco:C_B1 + fco + 1], 0.0, op0=OP.add, op1=OP.max),
                waits=deps(t_fm))
        for half in range(2):
            t_f2 = {}
            for fc in range(32):
                t_w2l = DMA("sync", lambda e, fc=fc: e.dma_start(
                    out=wqb[:, :], in_=bass.AP(wq2t_d.tensor, fc * 128 * D,
                                               [[D, 128], [1, 1024]])),
                    waits=deps(t_h, t_fe, TK("tensor")))
                for do4 in range(4):
                    do = half * 4 + do4
                    w0 = deps(t_w2l, t_h, t_fe) if do4 == 0 else ()
                    t_f2 = ST("tensor", lambda e, do=do, do4=do4, fc=fc: e.matmul(
                        pA[do4][:, :], wqb[:, do * 128:(do + 1) * 128],
                        hT[:, fc * 512:(fc + 1) * 512], start=(fc == 0), stop=(fc == 31)),
                        waits=w0)
            for do4 in range(4):
                do = half * 4 + do4
                t_fe = ST("vector", lambda e, do=do, do4=do4: e.tensor_scalar(
                    fT[:, do * 512:(do + 1) * 512], pA[do4][:, :],
                    consts[:, C_B2 + do:C_B2 + do + 1], None, op0=OP.add),
                    waits=deps(t_f2))
        for ti in range(4):
            t = q * 4 + ti
            t_xr = DMA("sync", lambda e, t=t: e.dma_start(
                out=x2t[:, :], in_=bass.AP(x2_d.tensor, t * 128 * D,
                                           [[D, 128], [1, D]])), waits=deps(t_od))
            for do in range(8):
                bnk = pB2p[do % 2]
                t_tp = ST("tensor", lambda e, do=do, ti=ti, bnk=bnk: e.transpose(
                    bnk[:, 0:128],
                    fT[:, do * 512 + ti * 128:do * 512 + ti * 128 + 128], identb[:, :]),
                    waits=deps(t_fe, t_oe2))
                t_oe2 = t_oe
                t_oe = ST("vector", lambda e, do=do, bnk=bnk: e.scalar_tensor_tensor(
                    outt[:, do * 128:(do + 1) * 128], bnk[:, 0:128], 1.0,
                    x2t[:, do * 128:(do + 1) * 128],
                    op0=OP.mult, op1=OP.add), waits=deps(t_tp, t_xr))
            t_od = DMA("sync", lambda e, t=t: e.dma_start(
                out=bass.AP(out_d.tensor, t * 128 * D, [[D, 128], [1, D]]),
                in_=outt[:, :]), sem="oq", waits=deps(t_oe, t_oe2))

    if dbg:
        dbg_xh = nc.dram_tensor("dbg_xh", [Tc + 1, DF], BF16, kind="ExternalOutput").ap()
        dbg_yT = nc.dram_tensor("dbg_yT", [DI, Tc], BF16, kind="ExternalOutput").ap()
        dbg_x2 = nc.dram_tensor("dbg_x2", [Tc, D], BF16, kind="ExternalOutput").ap()
        dbg_S = nc.dram_tensor("dbg_S", [NS, H * PD], F32, kind="ExternalOutput").ap()
        DMA("sync", lambda e: e.dma_start(out=dbg_xh, in_=xh_d), sem="oq", waits=deps(t_od))
        DMA("sync", lambda e: e.dma_start(out=dbg_yT, in_=yT_d), sem="oq", waits=deps(t_od))
        DMA("sync", lambda e: e.dma_start(out=dbg_x2, in_=x2_d), sem="oq", waits=deps(t_od))
        t_od = DMA("sync", lambda e: e.dma_start(out=dbg_S, in_=cc2i), sem="oq", waits=deps(t_od))
    ST("scalar", lambda e: e.nop(), waits=deps(t_od, TK("oq"), TK("dq"), TK("gq")))

    b.emit()
    return nc


def _host_inputs(x, W_in, W_out, W1, W2, b1, b2, A_log, dt_bias, Bb, Cb, NT):
    Tc = L * NT
    ident = np.eye(128)
    m_ls = np.tril(np.ones((128, 128), np.float32)).T * SCALE
    maskS_np = np.tile(m_ls, (1, 4)).astype(ml_dtypes.bfloat16)
    base = dict(
        W_in=W_in, W_out=W_out, W1=W1, W2=W2,
        b1c=np.ascontiguousarray(b1.reshape(32, 128).T.astype(np.float32)),
        b2c=np.ascontiguousarray(b2.reshape(8, 128).T.astype(np.float32)),
        dtb_rep=np.broadcast_to(dt_bias, (128, H)).copy().astype(np.float32),
        alog_rep=np.broadcast_to(A_log, (128, H)).copy().astype(np.float32),
        bbias_rep=np.broadcast_to(np.tile(Bb[0], 2), (128, 2 * NS)).copy().astype(np.float32),
        cbias_rep=np.broadcast_to(Cb[0], (128, NS)).copy().astype(np.float32),
        ident_bf=ident.astype(ml_dtypes.bfloat16),
        ident_f32=ident.astype(np.float32),
        ones_col=np.ones((128, 1), np.float32),
        onesb=np.ones((128, 128), ml_dtypes.bfloat16),
        maskS=maskS_np,
    )
    maps = []
    T = x.shape[0]
    n = T // Tc
    for k in range(n):
        xs = np.zeros((Tc + 1, D), np.float32)
        t0 = k * Tc
        if t0 > 0:
            xs[0] = x[t0 - 1]
        xs[1:] = x[t0:t0 + Tc]
        mP = (np.arange(8) < k).astype(np.float32).reshape(8, 1)
        mIT = np.zeros((8, 8), np.float32)
        for i in range(8):
            for j in range(8):
                mIT[i, j] = 1.0 if (j < i < k) else 0.0
        maps.append(dict(base, x_sl=xs, maskP=mP, maskIT=mIT))
    return maps


def kernel(L0_attn_x, L0_attn_W_in, L0_attn_A_log, L0_attn_dt_bias,
           L0_attn_B_norm_bias, L0_attn_C_norm_bias, L0_attn_W_out,
           L0_ffn_W1, L0_ffn_b1, L0_ffn_W2, L0_ffn_b2):
    x = np.ascontiguousarray(np.asarray(L0_attn_x, np.float32)[0])
    T = x.shape[0]
    NT = T // (N_CORES * L)
    nc = build_program(NT)
    maps = _host_inputs(x, np.ascontiguousarray(L0_attn_W_in, dtype=np.float32),
                        np.ascontiguousarray(L0_attn_W_out, dtype=np.float32),
                        np.ascontiguousarray(L0_ffn_W1, dtype=np.float32),
                        np.ascontiguousarray(L0_ffn_W2, dtype=np.float32),
                        np.asarray(L0_ffn_b1, np.float32),
                        np.asarray(L0_ffn_b2, np.float32),
                        np.asarray(L0_attn_A_log, np.float32),
                        np.asarray(L0_attn_dt_bias, np.float32),
                        np.asarray(L0_attn_B_norm_bias, np.float32),
                        np.asarray(L0_attn_C_norm_bias, np.float32), NT)
    res = run_bass_kernel_spmd(nc, maps, core_ids=list(range(N_CORES)))
    out = np.concatenate([r["out"] for r in res.results], axis=0)
    return out.reshape(1, T, D).astype(np.float32)


# version 05 (final)
# speedup vs baseline: 1.9997x; 1.9997x over previous
import math
import numpy as np
import ml_dtypes
import concourse.bass as bass
import concourse.mybir as mybir
from concourse.bass_utils import run_bass_kernel_spmd
from concourse.alu_op_type import AluOpType as OP

F32 = mybir.dt.float32
BF16 = mybir.dt.bfloat16
AF = mybir.ActivationFunctionType
AX = mybir.AxisListType.X

D = 1024
H = 32
NS = 64
PD = 64
DI = 2048
DP = 6432
DF = 4096
L = 128
SCALE = (NS * 2) ** -0.5
LNSCALE = math.log(SCALE)
PI = math.pi
OXR, OB = 2048, 6144
N_CORES = 8
ENGS = ("sync", "tensor", "vector", "scalar", "gpsimd")


def deps(*toks):
    w = {}
    for t in toks:
        if t:
            for k, v in t.items():
                w[k] = max(w.get(k, 0), v)
    return w


PARANOID = [{"ph1", "ph2", "ph3", "ph6"}]


class Builder:
    def __init__(self, nc):
        self.nc = nc
        self.items = {e: [] for e in ENGS}
        self.counts = {}
        self.sems = {}
        self.close_list = []

    def sem(self, name):
        cm = self.nc.semaphore(name)
        self.sems[name] = cm.__enter__()
        self.close_list.append(cm)
        self.counts[name] = 0

    phase = "ph0"

    def st(self, eng, fn, waits=(), sem=None, n=1):
        s = sem if sem is not None else eng
        if self.phase in PARANOID[0]:
            waits = dict(waits)
            for k, v in self.counts.items():
                waits[k] = max(waits.get(k, 0), v)
        self.counts[s] = self.counts.get(s, 0) + n
        self.items[eng].append((dict(waits), fn, s, n))
        return {s: self.counts[s]}

    def dma(self, eng, fn, waits=(), sem="dq"):
        return self.st(eng, fn, waits=waits, sem=sem, n=16)

    def tok(self, sem):
        return {sem: self.counts.get(sem, 0)}

    def sb(self, name, shape, dt=F32):
        cm = self.nc.sbuf_tensor(name, shape, dt)
        return cm, cm.__enter__()

    def ps(self, name, shape, dt=F32):
        cm = self.nc.psum_tensor(name, shape, dt)
        return cm, cm.__enter__()

    def free(self, *cms):
        for cm in cms:
            cm.__exit__(None, None, None)

    def emit(self):
        nc = self.nc
        sems = self.sems
        with nc.Block() as block:
            for eng in ENGS:
                def make(eng):
                    def body(e):
                        for waits, fn, s, n in self.items[eng]:
                            for k, v in waits.items():
                                e.wait_ge(sems[k], v)
                            inst = fn(e)
                            if inst is not None and s is not None:
                                inst.then_inc(sems[s], n)
                    return body
                body = make(eng)
                body.__name__ = "_"
                getattr(block, eng)(body)


def build_program(NT, dbg=False):
    assert NT % 4 == 0
    NCH = NT
    Tc = L * NT
    NQ = NT // 4
    nc = bass.Bass("TRN2", target_bir_lowering=False, detect_race_conditions=False)
    b = Builder(nc)
    for s in ("tensor", "vector", "scalar", "gpsimd", "dq", "gq", "cc", "oq"):
        b.sem(s)
    ST, DMA, TK = b.st, b.dma, b.tok

    def din(name, shape, dt=F32):
        return nc.dram_tensor(name, shape, dt, kind="ExternalInput").ap()

    def dint(name, shape, dt=F32):
        return nc.dram_tensor(name, shape, dt).ap()

    x_sl = din("x_sl", [Tc + 1, D])
    W_in = din("W_in", [D, DP])
    W_out = din("W_out", [DI, D])
    W1 = din("W1", [DF, D])
    W2 = din("W2", [D, DF])
    b1c = din("b1c", [128, 32])
    b2c = din("b2c", [128, 8])
    dtb_rep = din("dtb_rep", [128, H])
    alog_rep = din("alog_rep", [128, H])
    bbias_rep = din("bbias_rep", [128, 2 * NS])
    cbias_rep = din("cbias_rep", [128, NS])
    ident_bf = din("ident_bf", [128, 128], BF16)
    ident_f32 = din("ident_f32", [128, 128])
    ones_col = din("ones_col", [128, 1])
    onesb = din("onesb", [128, 128], BF16)
    maskS = din("maskS", [128, 4 * 128], BF16)
    maskP = din("maskP", [8, 1])
    maskIT = din("maskIT", [8, 8])
    out_d = nc.dram_tensor("out", [Tc, D], F32, kind="ExternalOutput").ap()

    xh_d = dint("xh_d", [Tc + 1, DF], BF16)
    yT_d = dint("yT_d", [DI, Tc], BF16)
    wq1t_d = dint("wq1t_d", [D, DF], BF16)
    wq2t_d = dint("wq2t_d", [DF, D], BF16)
    cumflat_d = dint("cumflat_d", [NCH, H * L])
    decflat_d = dint("decflat_d", [NCH, H * L], BF16)
    dec2flat_d = dint("dec2flat_d", [NCH, H * L], BF16)
    cdec_d = dint("cdec_d", [NCH, H])
    negO_d = dint("negO_d", [1, H])
    cose_d = dint("cose_d", [1, 2 * H])
    cf_d = dint("cf_d", [8, H])
    rs_d = dint("rs_d", [1, Tc])
    rq_d = dint("rq_d", [1, 2])
    cc1i = dint("cc1i", [1, 64])
    cc1o = dint("cc1o", [8, 64])
    x2_d = dint("x2_d", [Tc, D], BF16)
    cc2i = dint("cc2i", [NS, H * PD])
    cc2o = dint("cc2o", [8 * NS, H * PD])

    _, xnT = b.sb("xnT", [128, 8 * (NT + 1) * 128], BF16)
    _, lamdt_sb = b.sb("lamdt_sb", [128, NT * H], BF16)
    _, coeff_sb = b.sb("coeff_sb", [128, NT * H], BF16)
    _, CrT = b.sb("CrT", [64, Tc], BF16)
    _, ag1 = b.sb("ag1", [8, 64])
    _, consts = b.sb("consts", [128, 512])
    _, identb = b.sb("identb", [128, 128], BF16)
    _, onesb_s = b.sb("onesb_s", [128, 128], BF16)
    _, maskS_s = b.sb("maskS_s", [128, 512], BF16)
    _, mask8 = b.sb("mask8", [8, 9])
    _, stat = b.sb("stat", [128, 8])
    _, sm = b.sb("sm", [128, 128])
    _, negOc = b.sb("negOc", [32, 4])
    _, wT = b.sb("wT", [32, 128])
    _, red1 = b.sb("red1", [1, max(NT * H, 512)])
    _, cum_all = b.sb("cum_all", [32, Tc])
    _, CL_sb = b.sb("CL_sb", [32, NCH + 1])
    _, dec_hm = b.sb("dec_hm", [32, Tc], BF16)
    cm_Br, Br_sb = b.sb("Br_sb", [128, (NT + 1) * 128], BF16)
    cm_Bb, Bb_tm = b.sb("Bb_tm", [128, NT * 128], BF16)
    cm_BrT, BrT = b.sb("BrT", [64, 2 * (Tc + 1)], BF16)
    cm_ldA, ldA_sb = b.sb("ldA_sb", [128, NT * H])
    cm_cdR, cdecR = b.sb("cdecR", [64, NCH * H])
    cm_deT, deT_all = b.sb("deT_all", [128, NT * H], BF16)
    cm_cumT, cumT_all = b.sb("cumT_all", [128, NT * H])
    C_DTB, C_EXPA, C_BB, C_CB, C_B1, C_B2, C_ONE, C_IDF, C_COSE, C_RQ = (
        0, 32, 64, 192, 256, 288, 296, 300, 428, 492)

    cm_Bm, Bm_sb = b.sb("Bm_sb", [128, (NT + 1) * 128])
    cm_Cm, Cm_sb = b.sb("Cm_sb", [128, (NT + 1) * NS])
    cm_s3, s3_sb = b.sb("s3_sb", [128, NT * 96])
    cm_dt, dt_sb = b.sb("dt_sb", [128, NT * H])
    cm_w, w_sb = b.sb("w_sb", [128, NT * H])
    cm_ang, ang_sb = b.sb("ang_sb", [32, Tc])
    cm_cos, cos_tm = b.sb("cos_tm", [128, (NT + 1) * H])
    cm_sin, sin_tm = b.sb("sin_tm", [128, (NT + 1) * H])
    cm_xt, x_t = b.sb("x_t", [128, 1024])
    cm_ln, ln_t = b.sb("ln_t", [128, 1024])
    cm_xn, xn_t = b.sb("xn_t", [128, 1024], BF16)
    cm_wt, wtmp = b.sb("wtmp", [128, 1024])
    cm_xh, xh_s = b.sb("xh_s", [128, 1024], BF16)
    cm_ga, gacc = b.sb("gacc", [128, 66])
    cm_rp, rp = b.sb("rp", [128, 192])
    cm_Cr, Cr_sb = b.sb("Cr_sb", [128, (NT + 1) * NS], BF16)
    cm_W, W_sb = b.sb("W_sb", [128, 8 * 4384], BF16)

    cm_pA = []
    pA = []
    for i in range(4):
        cm, t = b.ps(f"pA{i}", [128, 512], F32)
        cm_pA.append(cm)
        pA.append(t)
    cm_pBa, pBa = b.ps("pB_bfa", [128, 1024], BF16)
    cm_pBb, pBb = b.ps("pB_bfb", [128, 1024], BF16)
    pBp = (pBa, pBb)
    cm_pCa, pCa = b.ps("pCa", [128, 512], F32)
    cm_pCb, pCb = b.ps("pCb", [128, 512], F32)
    pCp = (pCa, pCb)

    t_c = {}
    for dst, src in ((identb[:, :], ident_bf), (onesb_s[:, :], onesb),
                     (maskS_s[:, :], maskS), (mask8[:, 0:1], maskP),
                     (mask8[:, 1:9], maskIT),
                     (consts[:, C_DTB:C_DTB + 32], dtb_rep),
                     (consts[:, C_EXPA:C_EXPA + 32], alog_rep),
                     (consts[:, C_BB:C_BB + 128], bbias_rep),
                     (consts[:, C_CB:C_CB + 64], cbias_rep),
                     (consts[:, C_B1:C_B1 + 32], b1c),
                     (consts[:, C_B2:C_B2 + 8], b2c),
                     (consts[:, C_ONE:C_ONE + 1], ones_col),
                     (consts[:, C_IDF:C_IDF + 128], ident_f32)):
        t_c = DMA("sync", lambda e, dst=dst, src=src: e.dma_start(out=dst, in_=src))
    t_expa = ST("scalar", lambda e: e.activation(consts[:, C_EXPA:C_EXPA + 32],
                                                 consts[:, C_EXPA:C_EXPA + 32], AF.Exp),
                waits=deps(t_c))

    t_win = {}
    for fc in range(8):
        src = bass.AP(W_in.tensor, fc * 128 * DP + OXR, [[DP, 128], [1, 4384]])
        t_win = DMA("gpsimd", lambda e, fc=fc, src=src: e.dma_start(
            out=W_sb[:, fc * 4384:(fc + 1) * 4384], in_=src), sem="gq")

    b.phase = "ph1"
    NGRP = [(512 * i, 512) for i in range(8)] + [(4096, 288)]
    t_sm_t = [None] * (NT + 1)
    t_ev = {}
    t_mm = {}
    t_xd = {}
    t_tp_prev = {}
    t_te = {}
    t_te2 = {}
    for t in range(NT + 1):
        row0 = 1 + t * 128 if t < NT else 0
        t_x = DMA("sync", lambda e, row0=row0: e.dma_start(
            out=x_t[:, :], in_=bass.AP(x_sl.tensor, row0 * D, [[D, 128], [1, D]])),
            waits=deps(t_ev, TK("vector")))
        ST("vector", lambda e: e.tensor_reduce(stat[:, 0:1], x_t[:, :], axis=AX,
                                               op=OP.add), waits=deps(t_x))
        ST("vector", lambda e: e.tensor_scalar(stat[:, 1:2], stat[:, 0:1], 1.0 / D,
                                               None, op0=OP.mult))
        t_v = ST("vector", lambda e: e.tensor_scalar(ln_t[:, :], x_t[:, :],
                                                     stat[:, 1:2], None, op0=OP.subtract),
                 waits=deps(TK("vector")))
        if dbg and t == 0:
            dbg_x0 = nc.dram_tensor("dbg_x0", [128, 3 * 1024], F32, kind="ExternalOutput").ap()
            DMA("sync", lambda e: e.dma_start(
                out=bass.AP(dbg_x0.tensor, 0, [[3 * 1024, 128], [1, 1024]]),
                in_=x_t[:, :]), sem="oq", waits=deps(t_x, t_v))
            DMA("sync", lambda e: e.dma_start(
                out=bass.AP(dbg_x0.tensor, 1024, [[3 * 1024, 128], [1, 1024]]),
                in_=ln_t[:, :]), sem="oq", waits=deps(t_x, t_v))
        t_sq = ST("scalar", lambda e: e.activation(wtmp[:, :], ln_t[:, :], AF.Square),
                  waits=deps(t_v))
        if dbg and t == 0:
            t_dbq = DMA("sync", lambda e: e.dma_start(
                out=bass.AP(dbg_x0.tensor, 2048, [[3 * 1024, 128], [1, 1024]]),
                in_=wtmp[:, :]), sem="oq", waits=deps(t_sq))
        ST("vector", lambda e: e.tensor_reduce(stat[:, 2:3], wtmp[:, :], axis=AX,
                                               op=OP.add), waits=deps(t_sq))
        ST("vector", lambda e: e.tensor_scalar(stat[:, 3:4], stat[:, 2:3], 1.0 / D,
                                               1e-5, op0=OP.mult, op1=OP.add))
        t_rc = ST("vector", lambda e: e.reciprocal(stat[:, 4:5], stat[:, 3:4]))
        t_rs = ST("scalar", lambda e: e.activation(stat[:, 5:6], stat[:, 4:5], AF.Sqrt),
                  waits=deps(t_rc))
        t_xn = ST("vector", lambda e: e.tensor_scalar(xn_t[:, :], ln_t[:, :],
                                                      stat[:, 5:6], None, op0=OP.mult),
                  waits=deps(t_rs, t_tp_prev))
        for fc in range(8):
            bnk = pBp[fc % 2]
            t_tp = ST("tensor", lambda e, fc=fc, bnk=bnk: e.transpose(
                bnk[:, 0:128], xn_t[:, fc * 128:(fc + 1) * 128], identb[:, :]),
                waits=deps(t_xn, t_c, t_te2))
            t_te2 = t_te
            t_te = ST("vector", lambda e, fc=fc, t=t, bnk=bnk: e.tensor_copy(
                xnT[:, (fc * (NT + 1) + t) * 128:(fc * (NT + 1) + t) * 128 + 128],
                bnk[:, 0:128]), waits=deps(t_tp))
        t_tp_prev = t_tp
        t_xnT = deps(t_te, t_te2)
        for g, (c0, cw) in enumerate(NGRP):
            bank = pA[g % 3]
            for fc in range(8):
                b.phase = "ph1mm"
                t_mm = ST("tensor", lambda e, fc=fc, t=t, c0=c0, cw=cw, bank=bank: e.matmul(
                    bank[:, 0:cw],
                    xnT[:, (fc * (NT + 1) + t) * 128:(fc * (NT + 1) + t) * 128 + 128],
                    W_sb[:, fc * 4384 + c0:fc * 4384 + c0 + cw],
                    start=(fc == 0), stop=(fc == 7)),
                    waits=deps(t_xnT, t_win, t_ev, TK("scalar")) if fc == 0 else ())
                b.phase = "ph1"
            if g < 8:
                slot = (g % 2) * 512
                t_ev = ST("scalar", lambda e, g=g, bank=bank, slot=slot: e.activation(
                    xh_s[:, slot:slot + 512], bank[:, 0:512], AF.Silu),
                    waits=deps(t_mm, t_xd))
                if t < NT:
                    t_xd = DMA("sync", lambda e, t=t, g=g, slot=slot: e.dma_start(
                        out=bass.AP(xh_d.tensor, (1 + t * 128) * DF + g * 512,
                                    [[DF, 128], [1, 512]]),
                        in_=xh_s[:, slot:slot + 512]), waits=deps(t_ev))
                else:
                    t_xd = DMA("sync", lambda e, g=g, slot=slot: e.dma_start(
                        out=bass.AP(xh_d.tensor, g * 512, [[DF, 1], [1, 512]]),
                        in_=xh_s[0:1, slot:slot + 512]), waits=deps(t_ev))
            else:
                ST("scalar", lambda e, t=t, bank=bank: e.copy(
                    Bm_sb[:, t * 128:(t + 1) * 128], bank[:, 0:128]), waits=deps(t_mm))
                t_ev = ST("scalar", lambda e, t=t, bank=bank: e.copy(
                    Cm_sb[:, t * NS:(t + 1) * NS], bank[:, 128:192]))
                if t < NT:
                    t_ev = ST("scalar", lambda e, t=t, bank=bank: e.copy(
                        s3_sb[:, t * 96:(t + 1) * 96], bank[:, 192:288]))
                t_sm_t[t] = t_ev
    t_projmm = t_mm

    b.phase = "ph2"
    b.free(cm_W)
    cm_ch, cos_hm = b.sb("cos_hm", [32, Tc])
    cm_sh, sin_hm = b.sb("sin_hm", [32, Tc])
    cm_tr, trig = b.sb("trig", [32, Tc])
    cm_gt, gtmp = b.sb("gtmp", [128, 1024])
    cm_cdc, cdec_sb = b.sb("cdec_sb", [32, NCH])
    cm_wrA, wrA = b.sb("wrA", [32, Tc])
    cm_wrB, wrB = b.sb("wrB", [32, Tc])
    t_gz = ST("gpsimd", lambda e: e.memset(gacc[:, :], 0.0))
    gi = 0
    t_gred = t_gz
    for wsrc, ntile, wwidth in ((W1, 32, D), (W2, 8, DF)):
        for i in range(ntile):
            for piece in range(wwidth // 1024):
                src = bass.AP(wsrc.tensor, i * 128 * wwidth + piece * 1024,
                              [[wwidth, 128], [1, 1024]])
                t_gd = DMA("sync", lambda e, src=src: e.dma_start(out=gtmp[:, :], in_=src),
                           waits=deps(t_gred))
                t_gred = ST("vector", lambda e, gi=gi: e.tensor_reduce(
                    gacc[:, gi:gi + 1], gtmp[:, :], axis=AX, op=OP.add,
                    apply_absolute_value=True), waits=deps(t_gd, t_gz))
                gi += 1
    t_g1 = ST("vector", lambda e: e.tensor_reduce(gacc[:, 64:65], gacc[:, 0:32],
                                                  axis=AX, op=OP.add), waits=deps(t_gred))
    t_g2 = ST("vector", lambda e: e.tensor_reduce(gacc[:, 65:66], gacc[:, 32:64],
                                                  axis=AX, op=OP.add))
    t_gm = ST("tensor", lambda e: e.matmul(pA[2][0:1, 440:442], consts[:, C_ONE:C_ONE + 1],
                                           gacc[:, 64:66], start=True, stop=True),
              waits=deps(t_g1, t_g2, t_c))
    t_gs = ST("vector", lambda e: e.tensor_scalar(gacc[0:1, 64:66], pA[2][0:1, 440:442],
                                                  1.0 / (DF * D), 1e-5, op0=OP.mult,
                                                  op1=OP.add), waits=deps(t_gm))
    t_gr = ST("vector", lambda e: e.reciprocal(gacc[0:1, 64:66], gacc[0:1, 64:66]))
    t_gb = DMA("sync", lambda e: e.dma_start(out=rq_d, in_=gacc[0:1, 64:66]),
               waits=deps(t_gr))
    t_rq = DMA("gpsimd", lambda e: e.dma_start(
        out=consts[:, C_RQ:C_RQ + 2],
        in_=bass.AP(rq_d.tensor, 0, [[0, 128], [1, 2]])), sem="gq", waits=deps(t_gb))

    for t in range(NT):
        o = t * H
        s3o = t * 96
        ST("vector", lambda e, s3o=s3o: e.tensor_tensor(
            sm[:, 0:32], s3_sb[:, s3o:s3o + 32], consts[:, C_DTB:C_DTB + 32], op=OP.add),
            waits=deps(t_sm_t[t], t_expa))
        ST("scalar", lambda e: e.activation(sm[:, 32:64], sm[:, 0:32], AF.Abs),
           waits=deps(TK("vector")))
        t_sp0 = ST("scalar", lambda e: e.activation(sm[:, 64:96], sm[:, 32:64], AF.Exp,
                                                    scale=-1.0))
        t_sp1 = ST("vector", lambda e: e.tensor_scalar(sm[:, 96:128], sm[:, 64:96],
                                                       1.0, None, op0=OP.add),
                   waits=deps(t_sp0))
        t_sp2 = ST("scalar", lambda e: e.activation(sm[:, 32:64], sm[:, 96:128], AF.Ln),
                   waits=deps(t_sp1))
        t_sp3 = ST("vector", lambda e: e.tensor_scalar(sm[:, 96:128], sm[:, 0:32],
                                                       0.0, None, op0=OP.max),
                   waits=deps(t_sp2))
        t_sp = ST("vector", lambda e: e.tensor_tensor(sm[:, 32:64], sm[:, 32:64],
                                                      sm[:, 96:128], op=OP.add))
        ST("vector", lambda e, o=o: e.tensor_scalar(dt_sb[:, o:o + H], sm[:, 32:64],
                                                    1e-4, 0.5, op0=OP.max, op1=OP.min),
           waits=deps(t_sp))
        ST("vector", lambda e, o=o: e.tensor_tensor(
            sm[:, 0:32], dt_sb[:, o:o + H], consts[:, C_EXPA:C_EXPA + 32], op=OP.mult))
        ST("vector", lambda e, o=o: e.tensor_scalar(ldA_sb[:, o:o + H], sm[:, 0:32],
                                                    -1.0, None, op0=OP.mult))
        t_lm = ST("scalar", lambda e, s3o=s3o: e.activation(
            sm[:, 0:32], s3_sb[:, s3o + 32:s3o + 64], AF.Sigmoid),
            waits=deps(TK("vector")))
        ST("vector", lambda e, o=o: e.tensor_tensor(sm[:, 32:64], sm[:, 0:32],
                                                    dt_sb[:, o:o + H], op=OP.mult),
           waits=deps(t_lm))
        ST("vector", lambda e, o=o: e.tensor_copy(lamdt_sb[:, o:o + H], sm[:, 32:64]))
        ST("vector", lambda e, o=o: e.tensor_tensor(sm[:, 32:64], dt_sb[:, o:o + H],
                                                    sm[:, 32:64], op=OP.subtract))
        t_al = ST("scalar", lambda e, o=o: e.activation(sm[:, 0:32], ldA_sb[:, o:o + H],
                                                        AF.Exp), waits=deps(TK("vector")))
        ST("vector", lambda e: e.tensor_tensor(sm[:, 32:64], sm[:, 32:64], sm[:, 0:32],
                                               op=OP.mult), waits=deps(t_al))
        ST("vector", lambda e, o=o: e.tensor_copy(coeff_sb[:, o:o + H], sm[:, 32:64]))
        ST("vector", lambda e, o=o: e.tensor_reduce(sm[:, 0:1], dt_sb[:, o:o + H],
                                                    axis=AX, op=OP.add))
        ST("vector", lambda e: e.tensor_scalar(sm[:, 1:2], sm[:, 0:1], 1.0 / H, None,
                                               op0=OP.mult))
        t_w = ST("vector", lambda e, o=o, s3o=s3o: e.tensor_scalar(
            w_sb[:, o:o + H], s3_sb[:, s3o + 64:s3o + 96], sm[:, 1:2], None, op0=OP.mult),
            waits=deps(TK("vector")))
    t_wall = t_w

    t_scan = {}
    for t in range(NT):
        t_tw = ST("tensor", lambda e, t=t: e.transpose(
            pCa[0:32, 0:128], w_sb[:, t * H:(t + 1) * H], consts[:, C_IDF:C_IDF + 128]),
            waits=deps(t_wall, TK("vector")))
        ST("vector", lambda e: e.tensor_copy(wT[:, :], pCa[0:32, 0:128]),
           waits=deps(t_tw))
        ini = 0.0 if t == 0 else ang_sb[:, t * 128 - 1:t * 128]
        t_scan = ST("vector", lambda e, t=t, ini=ini: e.tensor_tensor_scan(
            ang_sb[:, t * 128:(t + 1) * 128], wT[:, :], wT[:, :], ini,
            op0=OP.add, op1=OP.bypass), waits=deps(TK("vector")))
        t_tl = ST("tensor", lambda e, t=t: e.transpose(
            pCb[0:32, 0:128], ldA_sb[:, t * H:(t + 1) * H],
            consts[:, C_IDF:C_IDF + 128]), waits=deps(t_scan))
        ST("vector", lambda e: e.tensor_copy(wT[:, :], pCb[0:32, 0:128]),
           waits=deps(t_tl))
        t_cum = ST("vector", lambda e, t=t: e.tensor_tensor_scan(
            cum_all[:, t * 128:(t + 1) * 128], wT[:, :], wT[:, :], 0.0,
            op0=OP.add, op1=OP.bypass))
        t_tc = ST("tensor", lambda e, t=t: e.transpose(
            pCa[:, 128:160], cum_all[:, t * 128:(t + 1) * 128],
            consts[0:32, C_IDF:C_IDF + 32]), waits=deps(t_cum))
        ST("vector", lambda e, t=t: e.tensor_copy(
            cumT_all[:, t * H:(t + 1) * H], pCa[:, 128:160]), waits=deps(t_tc))
        t_de = ST("scalar", lambda e, t=t: e.activation(
            wT[:, :], cum_all[:, t * 128:(t + 1) * 128], AF.Exp,
            bias=cum_all[:, t * 128 + 127:t * 128 + 128], scale=-1.0),
            waits=deps(TK("vector")))
        t_dt = ST("tensor", lambda e: e.transpose(
            pCb[:, 128:160], wT[:, :], consts[0:32, C_IDF:C_IDF + 32]), waits=deps(t_de))
        t_scan = ST("vector", lambda e, t=t: e.tensor_copy(
            deT_all[:, t * H:(t + 1) * H], pCb[:, 128:160]), waits=deps(t_dt))
    t_cumall = t_scan

    cum_last = cum_all[:, :].rearrange("p (c l) -> p c l", l=128)[:, :, 127:128].opt()
    t_cd = ST("scalar", lambda e: e.activation(cdec_sb[:, :], cum_last, AF.Exp),
              waits=deps(t_cumall))
    ST("vector", lambda e: e.memset(CL_sb[:, 0:1], 0.0), waits=deps(t_cumall))
    t_cl = ST("vector", lambda e: e.tensor_tensor_scan(
        CL_sb[:, 1:NCH + 1], cum_last, cum_last, 0.0, op0=OP.add, op1=OP.bypass))
    t_cdb = DMA("sync", lambda e: e.dma_start(
        out=bass.AP(cdec_d.tensor, 0, [[1, 32], [32, NCH]]), in_=cdec_sb[:, :],
        allow_slow_non_contiguous=True), waits=deps(t_cd))
    t_cdr = DMA("gpsimd", lambda e: e.dma_start(
        out=cdecR[:, :], in_=bass.AP(cdec_d.tensor, 0, [[0, 64], [1, NCH * 32]])),
        sem="gq", waits=deps(t_cdb))
    t_cfb = DMA("sync", lambda e: e.dma_start(
        out=bass.AP(cumflat_d.tensor, 0, [[128, 32], [H * L, NCH], [1, 128]]),
        in_=cum_all[:, :].rearrange("p (c l) -> p c l", l=128)), waits=deps(t_cumall))
    t_lns = ST("vector", lambda e: e.memset(negOc[0:32, 3:4], LNSCALE),
               waits=deps(t_cumall))
    t_ds = ST("scalar", lambda e: e.activation(dec_hm[:, :], cum_all[:, :], AF.Exp,
                                               bias=negOc[0:32, 3:4]),
              waits=deps(t_cumall, t_lns))
    t_dsb = DMA("sync", lambda e: e.dma_start(
        out=bass.AP(decflat_d.tensor, 0, [[128, 32], [H * L, NCH], [1, 128]]),
        in_=dec_hm[:, :].rearrange("p (c l) -> p c l", l=128)), waits=deps(t_ds))

    t_m1 = ST("tensor", lambda e: e.matmul(pA[2][0:1, 0:NT * H],
                                           consts[:, C_ONE:C_ONE + 1],
                                           w_sb[:, :NT * H], start=True, stop=True),
              waits=deps(t_wall, t_projmm, TK("vector")))
    t_e1 = ST("vector", lambda e: e.tensor_copy(red1[:, 0:NT * H],
                                                pA[2][0:1, 0:NT * H]), waits=deps(t_m1))
    t_sk = ST("vector", lambda e: e.tensor_reduce(
        ag1[0:1, 0:32], red1[0:1, 0:NT * H].rearrange("p (t i) -> p i t", i=32),
        axis=AX, op=OP.add), waits=deps(t_e1))
    t_m2 = ST("tensor", lambda e: e.matmul(pA[2][0:1, 0:NT * H],
                                           consts[:, C_ONE:C_ONE + 1],
                                           ldA_sb[:, :NT * H], start=True, stop=True),
              waits=deps(t_sk))
    t_e2 = ST("vector", lambda e: e.tensor_copy(red1[:, 0:NT * H],
                                                pA[2][0:1, 0:NT * H]), waits=deps(t_m2))
    t_ld = ST("vector", lambda e: e.tensor_reduce(
        ag1[0:1, 32:64], red1[0:1, 0:NT * H].rearrange("p (t i) -> p i t", i=32),
        axis=AX, op=OP.add), waits=deps(t_e2))
    t_d1 = DMA("sync", lambda e: e.dma_start(out=cc1i, in_=ag1[0:1, 0:64]),
               waits=deps(t_sk, t_ld))
    t_cc1 = ST("gpsimd", lambda e: e.collective_compute(
        "AllGather", OP.bypass, replica_groups=[list(range(N_CORES))],
        ins=[cc1i.opt()], outs=[cc1o.opt()]), waits=deps(t_d1), sem="cc")
    t_ag1 = DMA("sync", lambda e: e.dma_start(out=ag1[:, :], in_=cc1o),
                waits=deps(t_cc1))

    if dbg:
        dbg_xnT = nc.dram_tensor("dbg_xnT", [128, 8 * (NT + 1) * 128], BF16, kind="ExternalOutput").ap()
        DMA("sync", lambda e: e.dma_start(out=dbg_xnT, in_=xnT[:, :]), sem="oq",
            waits=deps(t_projmm, TK("vector")))
        dbg_s3 = nc.dram_tensor("dbg_s3", [128, NT * 96], F32, kind="ExternalOutput").ap()
        DMA("sync", lambda e: e.dma_start(out=dbg_s3, in_=s3_sb[:, :]), sem="oq",
            waits=deps(t_sm_t[NT - 1], t_sm_t[NT]))
        dbg_ag1 = nc.dram_tensor("dbg_ag1", [8, 64], F32, kind="ExternalOutput").ap()
        DMA("sync", lambda e: e.dma_start(out=dbg_ag1, in_=ag1[:, :]), sem="oq",
            waits=deps(t_ag1))
    t_ok = ST("tensor", lambda e: e.matmul(pA[2][0:1, 0:32], mask8[:, 0:1],
                                           ag1[:, 0:32], start=True, stop=True),
              waits=deps(t_ag1))
    t_no = ST("vector", lambda e: e.tensor_scalar(red1[:, 0:32], pA[2][0:1, 0:32],
                                                  -1.0, None, op0=OP.mult),
              waits=deps(t_ok))
    t_nb = DMA("sync", lambda e: e.dma_start(out=negO_d, in_=red1[0:1, 0:32]),
               waits=deps(t_no))
    t_nc = DMA("sync", lambda e: e.dma_start(
        out=negOc[:, 0:1], in_=bass.AP(negO_d.tensor, 0, [[1, 32], [1, 1]])),
        waits=deps(t_nb))


    def wrap_ladder(tgt, sa, sb, wv):
        t2 = wv
        for S in (128 * PI, 64 * PI, 32 * PI, 16 * PI, 8 * PI, 4 * PI, 2 * PI):
            ST("vector", lambda e, S=S, tgt=tgt, sa=sa: e.tensor_scalar(
                sa, tgt, S / 2, None, op0=OP.is_gt), waits=t2)
            ST("vector", lambda e, S=S, tgt=tgt, sb=sb: e.tensor_scalar(
                sb, tgt, -S / 2, None, op0=OP.is_lt))
            ST("vector", lambda e, sa=sa, sb=sb: e.tensor_tensor(sa, sa, sb,
                                                                 op=OP.subtract))
            ST("vector", lambda e, S=S, sa=sa: e.tensor_scalar(sa, sa, S, None,
                                                               op0=OP.mult))
            t2 = ST("vector", lambda e, tgt=tgt, sa=sa: e.tensor_tensor(
                tgt, tgt, sa, op=OP.subtract))
            t2 = deps(t2)
        return t2

    def trig_chain(shift, dst, wprev):
        ST("vector", lambda e: e.tensor_scalar(trig[:, :], ang_sb[:, :], -1.0, None,
                                               op0=OP.mult),
           waits=deps(t_cumall, t_nc, wprev))
        ST("vector", lambda e, shift=shift: e.tensor_scalar(
            trig[:, :], trig[:, :], negOc[:, 0:1], shift, op0=OP.add, op1=OP.add))
        t2 = wrap_ladder(trig[:, :], wrA[:, :], wrB[:, :], ())
        return ST("scalar", lambda e, dst=dst: e.activation(dst[:, :], trig[:, :],
                                                            AF.Sin), waits=deps(t2))

    t_sinh = trig_chain(0.0, sin_hm, {})
    t_cosh = trig_chain(PI / 2, cos_hm, t_sinh)
    t_x1 = ST("vector", lambda e: e.tensor_scalar(negOc[:, 1:2], negOc[:, 0:1], 0.0,
                                                  None, op0=OP.add), waits=deps(t_cosh))
    t_x1 = wrap_ladder(negOc[:, 1:2], wrA[:, 0:1], wrB[:, 0:1], deps(t_x1))
    t_x2b = ST("vector", lambda e: e.tensor_scalar(negOc[:, 2:3], negOc[:, 0:1],
                                                   PI / 2, None, op0=OP.add))
    t_x3 = wrap_ladder(negOc[:, 2:3], wrA[:, 0:1], wrB[:, 0:1], deps(t_x2b))
    t_se = ST("scalar", lambda e: e.activation(negOc[:, 1:2], negOc[:, 1:2], AF.Sin),
              waits=deps(t_x1, t_x3))
    t_ce = ST("scalar", lambda e: e.activation(negOc[:, 2:3], negOc[:, 2:3], AF.Sin))
    t_eb = DMA("sync", lambda e: e.dma_start(
        out=bass.AP(cose_d.tensor, 0, [[1, 32], [1, 1]]), in_=negOc[:, 2:3]),
        waits=deps(t_ce, t_se))
    t_eb = DMA("sync", lambda e: e.dma_start(
        out=bass.AP(cose_d.tensor, 32, [[1, 32], [1, 1]]), in_=negOc[:, 1:2]))
    t_er = DMA("gpsimd", lambda e: e.dma_start(
        out=consts[:, C_COSE:C_COSE + 64],
        in_=bass.AP(cose_d.tensor, 0, [[0, 128], [1, 64]])), sem="gq", waits=deps(t_eb))

    t_cs = {}
    for t in range(NT):
        t_t1 = ST("tensor", lambda e, t=t: e.transpose(
            pCa[:, 256:288], cos_hm[:, t * 128:(t + 1) * 128],
            consts[0:32, C_IDF:C_IDF + 32]), waits=deps(t_cosh, t_cs))
        t_cs = ST("vector", lambda e, t=t: e.tensor_copy(cos_tm[:, t * H:(t + 1) * H],
                                                         pCa[:, 256:288]),
                  waits=deps(t_t1))
        t_t2 = ST("tensor", lambda e, t=t: e.transpose(
            pCb[:, 256:288], sin_hm[:, t * 128:(t + 1) * 128],
            consts[0:32, C_IDF:C_IDF + 32]), waits=deps(t_cs))
        t_cs = ST("vector", lambda e, t=t: e.tensor_copy(sin_tm[:, t * H:(t + 1) * H],
                                                         pCb[:, 256:288]),
                  waits=deps(t_t2))
    t_cs = ST("vector", lambda e: e.tensor_copy(cos_tm[:, NT * H:(NT + 1) * H],
                                                consts[:, C_COSE:C_COSE + 32]),
              waits=deps(t_er, t_cs))
    t_cs = ST("vector", lambda e: e.tensor_copy(sin_tm[:, NT * H:(NT + 1) * H],
                                                consts[:, C_COSE + 32:C_COSE + 64]))

    t_rope = {}
    for t in range(NT + 1):
        ob = t * 128
        oc = t * NS
        t_q = ST("scalar", lambda e, ob=ob: e.activation(
            rp[:, 0:128], Bm_sb[:, ob:ob + 128], AF.Square),
            waits=deps(t_sm_t[t], t_cs, t_rope))
        ST("vector", lambda e: e.reduce_sum(stat[:, 0:1], rp[:, 0:64], axis=AX),
           waits=deps(t_q))
        ST("vector", lambda e: e.reduce_sum(stat[:, 1:2], rp[:, 64:128], axis=AX))
        t_q2 = ST("scalar", lambda e, oc=oc: e.activation(
            rp[:, 128:192], Cm_sb[:, oc:oc + NS], AF.Square), waits=deps(TK("vector")))
        ST("vector", lambda e: e.reduce_sum(stat[:, 2:3], rp[:, 128:192], axis=AX),
           waits=deps(t_q2))
        ST("vector", lambda e: e.tensor_scalar(stat[:, 0:3], stat[:, 0:3], 1.0 / NS,
                                               1e-6, op0=OP.mult, op1=OP.add))
        t_r1 = ST("vector", lambda e: e.reciprocal(stat[:, 0:3], stat[:, 0:3]))
        t_r2 = ST("scalar", lambda e: e.activation(stat[:, 4:7], stat[:, 0:3], AF.Sqrt),
                  waits=deps(t_r1))
        ST("vector", lambda e, ob=ob: e.tensor_scalar(rp[:, 0:64], Bm_sb[:, ob:ob + 64],
                                                      stat[:, 4:5], None, op0=OP.mult),
           waits=deps(t_r2))
        ST("vector", lambda e, ob=ob: e.tensor_scalar(rp[:, 64:128],
                                                      Bm_sb[:, ob + 64:ob + 128],
                                                      stat[:, 5:6], None, op0=OP.mult))
        ST("vector", lambda e: e.tensor_tensor(rp[:, 0:128], rp[:, 0:128],
                                               consts[:, C_BB:C_BB + 128], op=OP.mult))
        ST("vector", lambda e, oc=oc: e.tensor_scalar(
            rp[:, 128:192], Cm_sb[:, oc:oc + NS], stat[:, 6:7], None, op0=OP.mult))
        ST("vector", lambda e: e.tensor_tensor(rp[:, 128:192], rp[:, 128:192],
                                               consts[:, C_CB:C_CB + 64], op=OP.mult))
        co = cos_tm[:, t * H:(t + 1) * H]
        si = sin_tm[:, t * H:(t + 1) * H]
        for base in (0, 64, 128):
            ST("vector", lambda e, base=base, co=co: e.tensor_tensor(
                wtmp[:, base:base + 32], rp[:, base:base + 32], co, op=OP.mult))
            ST("vector", lambda e, base=base, si=si: e.tensor_tensor(
                wtmp[:, 192 + base:224 + base], rp[:, base + 32:base + 64], si,
                op=OP.mult))
            ST("vector", lambda e, base=base: e.tensor_tensor(
                wtmp[:, base:base + 32], wtmp[:, base:base + 32],
                wtmp[:, 192 + base:224 + base], op=OP.subtract))
            ST("vector", lambda e, base=base, si=si: e.tensor_tensor(
                wtmp[:, 224 + base:256 + base], rp[:, base:base + 32], si, op=OP.mult))
            ST("vector", lambda e, base=base, co=co: e.tensor_tensor(
                wtmp[:, 192 + base:224 + base], rp[:, base + 32:base + 64], co,
                op=OP.mult))
            ST("vector", lambda e, base=base: e.tensor_tensor(
                wtmp[:, base + 32:base + 64], wtmp[:, 224 + base:256 + base],
                wtmp[:, 192 + base:224 + base], op=OP.add))
        ST("vector", lambda e, ob=ob: e.tensor_copy(Br_sb[:, ob:ob + 128],
                                                    wtmp[:, 0:128]))
        t_rope = ST("vector", lambda e, oc=oc: e.tensor_copy(Cr_sb[:, oc:oc + NS],
                                                             wtmp[:, 128:192]))
    t_ropeall = t_rope

    if dbg:
        dbg_cos = nc.dram_tensor("dbg_cos", [128, (NT + 1) * H], F32, kind="ExternalOutput").ap()
        dbg_sin = nc.dram_tensor("dbg_sin", [128, (NT + 1) * H], F32, kind="ExternalOutput").ap()
        dbg_Br = nc.dram_tensor("dbg_Br", [128, (NT + 1) * 128], BF16, kind="ExternalOutput").ap()
        dbg_Cr = nc.dram_tensor("dbg_Cr", [128, (NT + 1) * NS], BF16, kind="ExternalOutput").ap()
        dbg_cum = nc.dram_tensor("dbg_cum", [32, Tc], F32, kind="ExternalOutput").ap()
        dbg_smalls = nc.dram_tensor("dbg_smalls", [128, NT * H * 4], F32, kind="ExternalOutput").ap()
        DMA("sync", lambda e: e.dma_start(out=dbg_cos, in_=cos_tm[:, :]), sem="oq", waits=deps(t_ropeall))
        DMA("sync", lambda e: e.dma_start(out=dbg_sin, in_=sin_tm[:, :]), sem="oq", waits=deps(t_ropeall))
        DMA("sync", lambda e: e.dma_start(out=dbg_Br, in_=Br_sb[:, :]), sem="oq", waits=deps(t_ropeall))
        DMA("sync", lambda e: e.dma_start(out=dbg_Cr, in_=Cr_sb[:, :]), sem="oq", waits=deps(t_ropeall))
        DMA("sync", lambda e: e.dma_start(out=dbg_cum, in_=cum_all[:, :]), sem="oq", waits=deps(t_ropeall))
        DMA("sync", lambda e: e.dma_start(
            out=bass.AP(dbg_smalls.tensor, 0, [[NT * H * 4, 128], [1, NT * H]]),
            in_=dt_sb[:, :]), sem="oq", waits=deps(t_ropeall))
        DMA("sync", lambda e: e.dma_start(
            out=bass.AP(dbg_smalls.tensor, NT * H, [[NT * H * 4, 128], [1, NT * H]]),
            in_=ldA_sb[:, :]), sem="oq", waits=deps(t_ropeall))
        DMA("gpsimd", lambda e: e.dma_start(
            out=bass.AP(dbg_smalls.tensor, 2 * NT * H, [[NT * H * 4, 128], [1, NT * H]]),
            in_=lamdt_sb[:, :]), sem="gq", waits=deps(t_ropeall))
        DMA("gpsimd", lambda e: e.dma_start(
            out=bass.AP(dbg_smalls.tensor, 3 * NT * H, [[NT * H * 4, 128], [1, NT * H]]),
            in_=coeff_sb[:, :]), sem="gq", waits=deps(t_ropeall))
    t_bt = {}
    t_bt2 = {}
    ib = [0]
    def alt():
        ib[0] ^= 1
        return pBp[ib[0]]
    for t in range(NT):
        for r in range(2):
            bnk = alt()
            t_t = ST("tensor", lambda e, t=t, r=r, bnk=bnk: e.transpose(
                bnk[0:64, 0:128], Br_sb[:, t * 128 + r * 64:t * 128 + r * 64 + 64],
                identb[:, :]), waits=deps(t_ropeall, t_bt2))
            t_bt2 = t_bt
            t_bt = ST("vector", lambda e, t=t, r=r, bnk=bnk: e.tensor_copy(
                BrT[:, r * (Tc + 1) + t * 128 + 1:r * (Tc + 1) + t * 128 + 129],
                bnk[0:64, 0:128]), waits=deps(t_t))
        bnk = alt()
        t_t = ST("tensor", lambda e, t=t, bnk=bnk: e.transpose(
            bnk[0:64, 0:128], Cr_sb[:, t * NS:(t + 1) * NS], identb[:, :]),
            waits=deps(t_bt2))
        t_bt2 = t_bt
        t_bt = ST("vector", lambda e, t=t, bnk=bnk: e.tensor_copy(
            CrT[:, t * 128:(t + 1) * 128], bnk[0:64, 0:128]), waits=deps(t_t))
    for r in range(2):
        bnk = alt()
        t_t = ST("tensor", lambda e, r=r, bnk=bnk: e.transpose(
            bnk[0:64, 0:1], Br_sb[0:1, NT * 128 + r * 64:NT * 128 + r * 64 + 64],
            identb[0:1, 0:1]), waits=deps(t_bt2))
        t_bt2 = t_bt
        t_bt = ST("vector", lambda e, r=r, bnk=bnk: e.tensor_copy(
            BrT[:, r * (Tc + 1):r * (Tc + 1) + 1], bnk[0:64, 0:1]), waits=deps(t_t))
    for c in range(NCH):
        for r in range(2):
            bnk = alt()
            t_t = ST("tensor", lambda e, c=c, r=r, bnk=bnk: e.transpose(
                bnk[:, 512:576],
                BrT[:, r * (Tc + 1) + c * 128:r * (Tc + 1) + c * 128 + 128],
                identb[0:64, 0:64]), waits=deps(t_bt2))
            t_bt2 = t_bt
            t_bt = ST("vector", lambda e, c=c, r=r, bnk=bnk: e.tensor_copy(
                Bb_tm[:, c * 128 + r * 64:c * 128 + r * 64 + 64], bnk[:, 512:576]),
                waits=deps(t_t))
    t_btall = deps(t_bt, t_bt2)

    b.phase = "ph3"
    b.free(cm_pCb, cm_pCa, cm_pBb, cm_pBa)
    b.free(cm_wrB, cm_wrA, cm_cdc, cm_gt, cm_tr, cm_sh, cm_ch)
    b.free(cm_W2del) if False else None
    b.free(cm_Cr, cm_rp, cm_ga, cm_xh, cm_wt, cm_xn, cm_ln, cm_xt,
           cm_sin, cm_cos, cm_ang, cm_w, cm_dt, cm_s3, cm_Cm, cm_Bm)
    cm_pQ = []
    pQ = []
    for i in range(4):
        cm, tns = b.ps(f"pQ{i}", [64, 512], F32)
        cm_pQ.append(cm)
        pQ.append(tns)
    cm_Mp, Mp_sb = b.sb("Mp_sb", [64, H * L], BF16)
    cm_D64, DEC64 = b.sb("DEC64", [64, H * L], BF16)
    _, Sin_bf = b.sb("Sin_bf", [64, DI], BF16)
    cm_Sc, Scur = b.sb("Scur", [64, DI])
    cm_Sp, Spv = b.sb("Spv", [64, DI], BF16)
    cm_xhg, xhg = b.sb("xhg", [128, DF], BF16)
    cm_xhb, xhb = b.sb("xhb", [128, DF], BF16)
    cm_Xg, Xg = b.sb("Xg", [128, 2 * DI], BF16)
    cm_Xb, Xb = b.sb("Xb", [128, 2 * DI], BF16)
    cm_RB, RB = b.sb("RB", [128, H * L])
    cm_E, E_sb = b.sb("E_sb", [128, H * L], BF16)
    cm_s, s_sb = b.sb("s_sb", [128, 512], BF16)
    cm_M, M_sb = b.sb("M_sb", [128, H * 4 * L], BF16)
    cm_ye, yev = b.sb("yev", [64, 512], BF16)
    dif = M_sb[:, 0:H * L * 2].bitcast(F32)

    t_z1 = ST("gpsimd", lambda e: e.memset(Scur[:, :], 0.0), waits=deps(t_btall))
    t_z2 = ST("gpsimd", lambda e: e.memset(Spv[:, :], 0.0))
    t_scan = deps(t_z1, t_z2)
    t_ydma = {}
    t_ycons = {}
    t_stmm = {}
    t_dfp = {}
    t_Mpp = {}
    t_Ep = {}
    for c in range(NCH):
        t_lg = DMA("sync", lambda e, c=c: e.dma_start(
            out=xhg[:, :], in_=bass.AP(xh_d.tensor, (1 + c * 128) * DF,
                                       [[DF, 128], [1, DF]])),
            waits=deps(t_ycons, t_stmm))
        t_lb = DMA("sync", lambda e, c=c: e.dma_start(
            out=xhb[:, :], in_=bass.AP(xh_d.tensor, c * 128 * DF,
                                       [[DF, 128], [1, DF]])))
        t_rb = DMA("gpsimd", lambda e, c=c: e.dma_start(
            out=RB[:, :],
            in_=bass.AP(cumflat_d.tensor, c * H * L, [[0, 128], [1, H * L]])),
            sem="gq", waits=deps(t_cfb, t_dfp))
        t_dc = DMA("gpsimd", lambda e, c=c: e.dma_start(
            out=DEC64[:, :],
            in_=bass.AP(decflat_d.tensor, c * H * L, [[0, 64], [1, H * L]])),
            sem="gq", waits=deps(t_dsb, t_Mpp))
        t_X = {}
        for r in range(2):
            xv = xhg[:, :].rearrange("p (h r q) -> p h r q", h=H, r=2)[:, :, r, :]
            lv = lamdt_sb[:, c * H:(c + 1) * H].unsqueeze(2).broadcast_to([128, H, PD])
            t_X = ST("vector", lambda e, r=r, xv=xv, lv=lv: e.tensor_tensor(
                Xg[:, r * DI:(r + 1) * DI].rearrange("p (h q) -> p h q", h=H),
                xv, lv, op=OP.mult), waits=deps(t_lg, t_ycons, t_stmm))
            xv2 = xhb[:, :].rearrange("p (h r q) -> p h r q", h=H, r=2)[:, :, r, :]
            cv = coeff_sb[:, c * H:(c + 1) * H].unsqueeze(2).broadcast_to([128, H, PD])
            t_X = ST("vector", lambda e, r=r, xv2=xv2, cv=cv: e.tensor_tensor(
                Xb[:, r * DI:(r + 1) * DI].rearrange("p (h q) -> p h q", h=H),
                xv2, cv, op=OP.mult), waits=deps(t_lb))
        t_sc = {}
        for j, (src, r) in enumerate(((1, 0), (1, 1), (0, 0), (0, 1))):
            lhs = BrT[:, r * (Tc + 1) + c * 128 + src:r * (Tc + 1) + c * 128 + src + 128]
            rhs = CrT[:, c * 128:(c + 1) * 128]
            b.phase = "ph3mm"
            t_sc = ST("tensor", lambda e, j=j, lhs=lhs, rhs=rhs: e.matmul(
                pA[0][:, j * 128:(j + 1) * 128], lhs, rhs, start=True, stop=True),
                waits=deps(t_btall, t_ycons, TK("vector")) if j == 0 else ())
            b.phase = "ph3"
        t_smk = ST("vector", lambda e: e.tensor_tensor(s_sb[:, :], pA[0][:, :],
                                                       maskS_s[:, :], op=OP.mult),
                   waits=deps(t_sc))
        cv2 = cumT_all[:, c * H:(c + 1) * H].unsqueeze(2).broadcast_to([128, H, L])
        t_df = ST("vector", lambda e, cv2=cv2: e.scalar_tensor_tensor(
            dif.rearrange("p (h l) -> p h l", h=H),
            RB[:, :].rearrange("p (h l) -> p h l", h=H), 0.0, cv2,
            op0=OP.add, op1=OP.subtract), waits=deps(t_rb, t_Ep, t_ycons))
        t_dfp = t_df
        t_mn = ST("vector", lambda e: e.tensor_scalar(dif, dif, 0.0, None,
                                                      op0=OP.min))
        t_E = ST("scalar", lambda e: e.activation(E_sb[:, :], dif, AF.Exp),
                 waits=deps(t_mn, t_ycons))
        t_Ep = t_E
        sv = s_sb[:, :].rearrange("p (s l) -> p s l", s=4).unsqueeze(1).broadcast_to([128, H, 4, L])
        ev = E_sb[:, :].rearrange("p (h l) -> p h l", h=H).unsqueeze(2).broadcast_to([128, H, 4, L])
        t_M = ST("vector", lambda e, sv=sv, ev=ev: e.tensor_tensor(
            M_sb[:, :].rearrange("p (h s l) -> p h s l", h=H, s=4), sv, ev,
            op=OP.mult), waits=deps(t_smk, t_E, t_ycons))
        cb = CrT[:, c * 128:(c + 1) * 128].unsqueeze(1).broadcast_to([64, H, L])
        t_Mp = ST("vector", lambda e, cb=cb: e.tensor_tensor(
            Mp_sb[:, :].rearrange("p (h l) -> p h l", h=H), cb,
            DEC64[:, :].rearrange("p (h l) -> p h l", h=H), op=OP.mult),
            waits=deps(t_dc, t_ycons))
        t_Mpp = t_Mp
        t_spv = ST("vector", lambda e: e.tensor_copy(Spv[:, :], Scur[:, :]),
                   waits=deps(t_scan, t_ycons))
        t_ymm = {}
        t_wave = {}
        for wv in range(2):
            for hh in range(16):
                h = wv * 16 + hh
                bank = pA[hh // 4]
                col = (hh % 4) * 128
                b.phase = "ph3mm"
                for j, (XT, r) in enumerate(((Xg, 0), (Xg, 1), (Xb, 0), (Xb, 1))):
                    lhs = XT[:, r * DI + h * PD:r * DI + (h + 1) * PD]
                    rhs = M_sb[:, (h * 4 + j) * 128:(h * 4 + j + 1) * 128]
                    w0 = deps(t_M, t_X, t_spv, t_Mp, t_wave) if (hh == 0 and j == 0) else ()
                    t_ymm = ST("tensor", lambda e, bank=bank, col=col, lhs=lhs, rhs=rhs,
                               j=j: e.matmul(bank[0:64, col:col + 128], lhs, rhs,
                                             start=(j == 0), stop=False), waits=w0)
                lhs5 = Spv[:, h * PD:(h + 1) * PD]
                rhs5 = Mp_sb[:, h * 128:(h + 1) * 128]
                t_ymm = ST("tensor", lambda e, bank=bank, col=col, lhs5=lhs5, rhs5=rhs5:
                           e.matmul(bank[0:64, col:col + 128], lhs5, rhs5,
                                    start=False, stop=True))
                b.phase = "ph3"
            for bk in range(4):
                h0 = wv * 16 + bk * 4
                t_ye = ST("vector", lambda e, bk=bk: e.tensor_copy(
                    yev[:, :], pA[bk][0:64, :]), waits=deps(t_ymm, t_ydma))
                t_ydma = DMA("sync", lambda e, h0=h0, c=c: e.dma_start(
                    out=bass.AP(yT_d.tensor, h0 * PD * Tc + c * 128,
                                [[Tc, 64], [PD * Tc, 4], [1, 128]]),
                    in_=yev[:, :].rearrange("p (h l) -> p h l", h=4)),
                    waits=deps(t_ye))
            t_wave = deps(t_ydma)
        t_ycons = deps(t_ymm, t_ydma)
        t_X2 = {}
        for XT in (Xg, Xb):
            dv = deT_all[:, c * H:(c + 1) * H].unsqueeze(2).broadcast_to([128, H, PD])
            for r in range(2):
                t_X2 = ST("gpsimd", lambda e, XT=XT, r=r, dv=dv: e.tensor_tensor(
                    XT[:, r * DI:(r + 1) * DI].rearrange("p (h q) -> p h q", h=H),
                    XT[:, r * DI:(r + 1) * DI].rearrange("p (h q) -> p h q", h=H),
                    dv, op=OP.mult), waits=deps(t_ymm, t_z2) if XT is Xg and r == 0 else ())
        t_st = {}
        for j in range(4):
            r = j % 2
            XT = Xg if j < 2 else Xb
            if j < 2:
                lhs = Br_sb[:, c * 128 + r * 64:c * 128 + r * 64 + 64]
            else:
                lhs = Bb_tm[:, c * 128 + r * 64:c * 128 + r * 64 + 64]
            for nt4 in range(4):
                rhs = XT[:, r * DI + nt4 * 512:r * DI + (nt4 + 1) * 512]
                w0 = deps(t_X2, t_scan) if (j == 0 and nt4 == 0) else ()
                b.phase = "ph3mm"
                t_st = ST("tensor", lambda e, lhs=lhs, rhs=rhs, nt4=nt4, j=j: e.matmul(
                    pQ[nt4][:, :], lhs, rhs, start=(j == 0), stop=(j == 3)), waits=w0)
                b.phase = "ph3"
        t_stmm = t_st
        cdv = cdecR[:, c * H:(c + 1) * H].unsqueeze(2).broadcast_to([64, H, PD])
        t_s1 = ST("gpsimd", lambda e, cdv=cdv: e.tensor_tensor(
            Scur[:, :].rearrange("p (h q) -> p h q", h=H),
            Scur[:, :].rearrange("p (h q) -> p h q", h=H), cdv, op=OP.mult),
            waits=deps(t_spv, t_cdr, t_X2))
        t_sca = {}
        for nt4 in range(4):
            t_sca = ST("vector", lambda e, nt4=nt4: e.tensor_tensor(
                Scur[:, nt4 * 512:(nt4 + 1) * 512], Scur[:, nt4 * 512:(nt4 + 1) * 512],
                pQ[nt4][:, :], op=OP.add), waits=deps(t_s1, t_st))
        t_scan = t_sca

    t_sd = DMA("sync", lambda e: e.dma_start(out=cc2i, in_=Scur[:, :]),
               waits=deps(t_scan))
    t_cc2 = ST("gpsimd", lambda e: e.collective_compute(
        "AllGather", OP.bypass, replica_groups=[list(range(N_CORES))],
        ins=[cc2i.opt()], outs=[cc2o.opt()]), waits=deps(t_sd), sem="cc")

    b.phase = "ph4"
    b.free(cm_ye, cm_M, cm_s, cm_E, cm_RB, cm_Xb, cm_Xg, cm_xhb, cm_xhg)
    for cm in reversed(cm_pQ):
        b.free(cm)
    cm_pB2a, pB2a = b.ps("pB2a", [128, 1024], BF16)
    cm_pB2b, pB2b = b.ps("pB2b", [128, 1024], BF16)
    pB2p = (pB2a, pB2b)
    cm_qw, qw = b.sb("qw", [128, 1024])
    cm_qv, qv = b.sb("qv", [128, 1024])
    cm_qa, qa = b.sb("qa", [128, 1024])
    cm_qt, qt = b.sb("qt", [128, 1024], BF16)
    cm_qtT, qtT = b.sb("qtT", [128, 1024], BF16)
    t_qdma = {}
    t_qred = {}
    for widx, (wsrc, wdst, ntile, wwidth, ddim) in enumerate(
            ((W1, wq1t_d, 32, D, DF), (W2, wq2t_d, 8, DF, D))):
        rqc = consts[:, C_RQ + widx:C_RQ + widx + 1]
        for i in range(ntile):
            for piece in range(wwidth // 1024):
                src = bass.AP(wsrc.tensor, i * 128 * wwidth + piece * 1024,
                              [[wwidth, 128], [1, 1024]])
                t_wl = DMA("sync", lambda e, src=src: e.dma_start(out=qw[:, :], in_=src),
                           waits=deps(t_qred, t_rq, TK("tensor"), TK("vector"),
                                      TK("scalar")) if not t_qred else deps(t_qred, t_rq))
                ST("vector", lambda e, rqc=rqc: e.tensor_scalar(
                    qv[:, :], qw[:, :], rqc, None, op0=OP.mult), waits=deps(t_wl))
                ST("vector", lambda e: e.tensor_scalar(qa[:, :], qv[:, :], 0.5, None,
                                                       op0=OP.is_gt))
                ST("vector", lambda e: e.tensor_scalar(qv[:, :], qv[:, :], -0.5, None,
                                                       op0=OP.is_lt))
                t_qq = ST("vector", lambda e: e.tensor_tensor(qt[:, :], qa[:, :],
                                                              qv[:, :], op=OP.subtract))
                t_qred = t_qq
                t_qe = {}
                t_qe2 = {}
                for dc in range(8):
                    bnk = pB2p[dc % 2]
                    t_qt2 = ST("tensor", lambda e, dc=dc, bnk=bnk: e.transpose(
                        bnk[:, 0:128], qt[:, dc * 128:(dc + 1) * 128],
                        identb[:, :]), waits=deps(t_qq, t_qe2))
                    t_qe2 = t_qe
                    t_qe = ST("vector", lambda e, dc=dc, bnk=bnk: e.tensor_copy(
                        qtT[:, dc * 128:(dc + 1) * 128], bnk[:, 0:128]),
                        waits=deps(t_qt2))
                t_qe = deps(t_qe, t_qe2)
                dst = bass.AP(wdst.tensor, (piece * 1024) * ddim + i * 128,
                              [[ddim, 128], [128 * ddim, 8], [1, 128]])
                t_qdma = DMA("sync", lambda e, dst=dst: e.dma_start(
                    out=dst, in_=qtT[:, :].rearrange("p (dc q) -> p dc q", dc=8)),
                    waits=deps(t_qe))

    cm_cfR, cfR = b.sb("cfR", [64, 8 * H])
    cm_Sj, Sj = b.sb("Sj", [64, DI])
    cm_St2, St2 = b.sb("St2", [64, DI])
    cm_Sin, Sin = b.sb("Sin", [64, DI])
    t_w8 = ST("tensor", lambda e: e.matmul(pA[0][0:8, 0:32], mask8[:, 1:9],
                                           ag1[:, 32:64], start=True, stop=True),
              waits=deps(t_ag1, t_ycons, t_qdma))
    t_cf = ST("scalar", lambda e: e.activation(wT[0:8, 0:32], pA[0][0:8, 0:32], AF.Exp),
              waits=deps(t_w8))
    t_cf = ST("vector", lambda e: e.tensor_scalar(wT[0:8, 0:32], wT[0:8, 0:32],
                                                  mask8[:, 0:1], None, op0=OP.mult),
              waits=deps(t_cf, t_qred))
    t_cfd = DMA("sync", lambda e: e.dma_start(out=cf_d, in_=wT[0:8, 0:32]),
                waits=deps(t_cf))
    t_cfr = DMA("gpsimd", lambda e: e.dma_start(
        out=cfR[:, :], in_=bass.AP(cf_d.tensor, 0, [[0, 64], [1, 8 * H]])), sem="gq",
        waits=deps(t_cfd))
    t_si = ST("gpsimd", lambda e: e.memset(Sin[:, :], 0.0))
    t_sa = {}
    for j in range(8):
        t_sl = DMA("sync", lambda e, j=j: e.dma_start(
            out=Sj[:, :], in_=bass.AP(cc2o.tensor, j * NS * H * PD,
                                      [[H * PD, NS], [1, H * PD]])),
            waits=deps(t_cc2, t_sa))
        jv = cfR[:, j * H:(j + 1) * H].unsqueeze(2).broadcast_to([64, H, PD])
        t_sm2 = ST("vector", lambda e, jv=jv: e.tensor_tensor(
            St2[:, :].rearrange("p (h q) -> p h q", h=H),
            Sj[:, :].rearrange("p (h q) -> p h q", h=H), jv, op=OP.mult),
            waits=deps(t_sl, t_cfr, t_si))
        t_sa = ST("vector", lambda e: e.tensor_tensor(Sin[:, :], Sin[:, :], St2[:, :],
                                                      op=OP.add))
    t_sbf = ST("vector", lambda e: e.tensor_copy(Sin_bf[:, :], Sin[:, :]),
               waits=deps(t_sa))
    t_d2 = {}
    for c in range(NCH):
        t_bc = ST("vector", lambda e, c=c: e.tensor_scalar(
            negOc[:, 3:4], CL_sb[:, c:c + 1], 1.0, LNSCALE, op0=OP.mult, op1=OP.add),
            waits=deps(t_cl, t_sbf, t_d2))
        t_dh = ST("scalar", lambda e, c=c: e.activation(
            dec_hm[:, c * 128:(c + 1) * 128], cum_all[:, c * 128:(c + 1) * 128],
            AF.Exp, bias=negOc[:, 3:4]), waits=deps(t_bc))
        t_d2 = DMA("sync", lambda e, c=c: e.dma_start(
            out=bass.AP(dec2flat_d.tensor, c * H * L, [[128, 32], [1, 128]]),
            in_=dec_hm[:, c * 128:(c + 1) * 128]), waits=deps(t_dh))

    b.phase = "ph5"
    b.free(cm_Sin, cm_St2, cm_Sj, cm_cfR, cm_qtT, cm_qt, cm_qa, cm_qv, cm_qw)
    b.free(cm_Sp, cm_Sc)
    _, x2t = b.sb("x2t", [128, 1024], BF16)
    _, rsc = b.sb("rsc", [128, 4])
    cm_x3, x_t3 = b.sb("x_t3", [128, 1024])
    cm_g2, gtmp2 = b.sb("gtmp2", [128, 512])
    cm_Wz, Wz = b.sb("Wz", [128, 8 * DI], BF16)
    cm_Wo, Wo = b.sb("Wo", [128, 2 * 16 * 128], BF16)
    cm_zs, zs = b.sb("zs", [128, 16 * 512], BF16)
    cm_gs, gsb = b.sb("gsb", [128, 16 * 512], BF16)
    cm_ys, ysb = b.sb("ysb", [128, DI], BF16)
    cm_yes, yesb = b.sb("yesb", [128, DI], BF16)
    cm_PT, PT = b.sb("PT", [128, 8 * 512], BF16)
    t_wz = {}
    for fc in range(8):
        src = bass.AP(W_in.tensor, fc * 128 * DP, [[DP, 128], [1, DI]])
        t_wz = DMA("gpsimd", lambda e, fc=fc, src=src: e.dma_start(
            out=Wz[:, fc * DI:(fc + 1) * DI], in_=src), sem="gq",
            waits=deps(t_cfr, t_sbf, t_d2))
    t_wo = {}

    t_x2done = {}
    t_prev = {}
    t_tev = {}
    t_tev2 = {}
    for tg in range(NQ):
        t_zev = {}
        for fcz in range(16):
            bank = pA[fcz % 2]
            t_zmm = {}
            for dc in range(8):
                w0 = deps(t_wz, t_zev, t_w8, t_prev, t_sbf, TK("scalar")) if dc == 0 else ()
                t_zmm = ST("tensor", lambda e, fcz=fcz, dc=dc, tg=tg, bank=bank: e.matmul(
                    bank[:, :], Wz[:, dc * DI + fcz * 128:dc * DI + fcz * 128 + 128],
                    xnT[:, (dc * (NT + 1) + tg * 4) * 128:(dc * (NT + 1) + tg * 4) * 128 + 512],
                    start=(dc == 0), stop=(dc == 7)), waits=w0)
            t_zev = ST("scalar", lambda e, fcz=fcz, bank=bank: e.activation(
                zs[:, fcz * 512:(fcz + 1) * 512], bank[:, :], AF.Silu),
                waits=deps(t_zmm))
        t_ge = {}
        for ci in range(4):
            c = tg * 4 + ci
            t_dcl = DMA("gpsimd", lambda e, c=c: e.dma_start(
                out=DEC64[:, :],
                in_=bass.AP(dec2flat_d.tensor, c * H * L, [[0, 64], [1, H * L]])),
                sem="gq", waits=deps(t_d2, t_ge, t_Mpp, t_wo))
            cb = CrT[:, c * 128:(c + 1) * 128].unsqueeze(1).broadcast_to([64, H, L])
            t_mpp = ST("vector", lambda e, cb=cb: e.tensor_tensor(
                Mp_sb[:, :].rearrange("p (h l) -> p h l", h=H), cb,
                DEC64[:, :].rearrange("p (h l) -> p h l", h=H), op=OP.mult),
                waits=deps(t_dcl, t_sbf))
            t_Mpp = t_mpp
            t_yl = DMA("sync", lambda e, c=c: e.dma_start(
                out=ysb[:, :], in_=bass.AP(yT_d.tensor, c * 128,
                                           [[Tc, 128], [128 * Tc, 16], [1, 128]])),
                waits=deps(t_ge, t_sbf))
            t_cm = {}
            for h in range(H):
                bank = pA[h // 8]
                row = (h % 2) * 64
                col = ((h // 2) % 4) * 128
                tp = (0, 64) if row else None
                lhs = Sin_bf[:, h * PD:(h + 1) * PD]
                rhs = Mp_sb[:, h * 128:(h + 1) * 128]
                w0 = deps(t_mpp, t_zev, t_ge) if h == 0 else ()
                t_cm = ST("tensor", lambda e, bank=bank, row=row, col=col, tp=tp,
                          lhs=lhs, rhs=rhs: e.matmul(
                    bank[row:row + 64, col:col + 128], lhs, rhs, start=True, stop=True,
                    tile_position=tp), waits=w0)
            t_yee = {}
            for bk in range(4):
                yv = ysb[:, :].rearrange("p (g l) -> p g l", g=16)[:, bk * 4:(bk + 1) * 4, :]
                t_yee = ST("vector", lambda e, bk=bk, yv=yv: e.tensor_tensor(
                    yesb[:, bk * 512:(bk + 1) * 512].rearrange("p (g l) -> p g l", g=4),
                    pA[bk][:, :].rearrange("p (g l) -> p g l", g=4), yv, op=OP.add),
                    waits=deps(t_cm, t_yl))
            for fc in range(16):
                t_ge = ST("vector", lambda e, fc=fc, ci=ci: e.tensor_tensor(
                    gsb[:, fc * 512 + ci * 128:fc * 512 + ci * 128 + 128],
                    yesb[:, fc * 128:(fc + 1) * 128],
                    zs[:, fc * 512 + ci * 128:fc * 512 + ci * 128 + 128], op=OP.mult),
                    waits=deps(t_yee, t_zev))
        t_sqm = {}
        t_sq = {}
        for fc in range(16):
            t_sq = ST("scalar", lambda e, fc=fc: e.activation(
                gtmp2[:, 0:512], gsb[:, fc * 512:(fc + 1) * 512],
                AF.Square), waits=deps(t_ge, t_sqm))
            t_sqm = ST("tensor", lambda e, fc=fc: e.matmul(
                pA[0][0:1, :], consts[:, C_ONE:C_ONE + 1], gtmp2[:, 0:512],
                start=(fc == 0), stop=(fc == 15)), waits=deps(t_sq))
        t_rr = ST("vector", lambda e: e.tensor_scalar(
            red1[0:1, 0:512], pA[0][0:1, :], 1.0 / DI, 1e-6, op0=OP.mult, op1=OP.add),
            waits=deps(t_sqm))
        t_rr = ST("vector", lambda e: e.reciprocal(red1[0:1, 0:512], red1[0:1, 0:512]))
        t_rr = ST("scalar", lambda e: e.activation(red1[0:1, 0:512], red1[0:1, 0:512],
                                                   AF.Sqrt), waits=deps(t_rr))
        t_rd = DMA("sync", lambda e, tg=tg: e.dma_start(
            out=bass.AP(rs_d.tensor, tg * 512, [[512, 1], [1, 512]]),
            in_=red1[0:1, 0:512]), waits=deps(t_rr))
        t_pt = {}
        for do in range(8):
            wslot = (do % 2) * 2048
            wsrc = bass.AP(W_out.tensor, do * 128, [[D, 128], [128 * D, 16], [1, 128]])
            t_wo = DMA("gpsimd", lambda e, wslot=wslot, wsrc=wsrc: e.dma_start(
                out=Wo[:, wslot:wslot + 2048].rearrange("p (di q) -> p di q", di=16),
                in_=wsrc), sem="gq", waits=deps(t_wo, TK("tensor")))
            bank = pA[1 + (do % 2)]
            t_wm = {}
            for di in range(16):
                w0 = deps(t_ge, t_wo, t_pt, t_rr) if di == 0 else ()
                t_wm = ST("tensor", lambda e, di=di, bank=bank, wslot=wslot: e.matmul(
                    bank[:, :], Wo[:, wslot + di * 128:wslot + di * 128 + 128],
                    gsb[:, di * 512:(di + 1) * 512], start=(di == 0), stop=(di == 15)),
                    waits=w0)
            t_pt = ST("vector", lambda e, do=do, bank=bank: e.tensor_copy(
                PT[:, do * 512:(do + 1) * 512], bank[:, :]), waits=deps(t_wm))
        for ti in range(4):
            t = tg * 4 + ti
            t_rl = DMA("sync", lambda e, t=t: e.dma_start(
                out=rsc[:, 0:1], in_=bass.AP(rs_d.tensor, t * 128, [[1, 128], [1, 1]])),
                waits=deps(t_rd))
            t_xr = DMA("sync", lambda e, t=t: e.dma_start(
                out=x_t3[:, :], in_=bass.AP(x_sl.tensor, (1 + t * 128) * D,
                                            [[D, 128], [1, D]])),
                waits=deps(t_x2done, t_sbf))
            for do in range(8):
                bnk = pB2p[do % 2]
                t_tp = ST("tensor", lambda e, do=do, ti=ti, bnk=bnk: e.transpose(
                    bnk[:, 0:128],
                    PT[:, do * 512 + ti * 128:do * 512 + ti * 128 + 128], identb[:, :]),
                    waits=deps(t_pt, t_tev2))
                t_tev2 = t_tev
                t_tev = ST("vector", lambda e, do=do, bnk=bnk: e.scalar_tensor_tensor(
                    x2t[:, do * 128:do * 128 + 128],
                    bnk[:, 0:128], rsc[:, 0:1],
                    x_t3[:, do * 128:(do + 1) * 128], op0=OP.mult, op1=OP.add),
                    waits=deps(t_tp, t_rl, t_xr))
            t_x2done = DMA("sync", lambda e, t=t: e.dma_start(
                out=bass.AP(x2_d.tensor, t * 128 * D, [[D, 128], [1, D]]),
                in_=x2t[:, :]), waits=deps(t_tev, t_tev2))
        t_prev = deps(t_ge, t_rr, t_pt, t_x2done)

    b.phase = "ph6"
    b.free(cm_PT, cm_yes, cm_ys, cm_gs, cm_zs, cm_Wo, cm_Wz, cm_g2)
    _, xn2T = b.sb("xn2T", [128, 8 * Tc], BF16)
    cm_l1, lt2 = b.sb("lt2", [128, 1024])
    cm_l2, lw2 = b.sb("lw2", [128, 1024])
    cm_l3, ln2n = b.sb("ln2n", [128, 1024], BF16)
    cm_l4, x2l = b.sb("x2l", [128, 1024], BF16)
    t_l2 = {}
    t_te = {}
    t_te2 = {}
    for t in range(NT):
        t_xl2 = DMA("sync", lambda e, t=t: e.dma_start(
            out=x2l[:, :], in_=bass.AP(x2_d.tensor, t * 128 * D, [[D, 128], [1, D]])),
            waits=deps(t_x2done, t_l2))
        ST("vector", lambda e: e.tensor_reduce(
            stat[:, 0:1], x2l[:, :], axis=AX, op=OP.add), waits=deps(t_xl2))
        ST("vector", lambda e: e.tensor_scalar(stat[:, 1:2], stat[:, 0:1], 1.0 / D,
                                               None, op0=OP.mult))
        t_v = ST("vector", lambda e: e.tensor_scalar(
            lt2[:, :], x2l[:, :], stat[:, 1:2], None, op0=OP.subtract),
            waits=deps(TK("vector")))
        t_s = ST("scalar", lambda e: e.activation(lw2[:, :], lt2[:, :], AF.Square),
                 waits=deps(t_v))
        ST("vector", lambda e: e.tensor_reduce(stat[:, 2:3], lw2[:, :], axis=AX,
                                               op=OP.add), waits=deps(t_s))
        ST("vector", lambda e: e.tensor_scalar(stat[:, 3:4], stat[:, 2:3], 1.0 / D,
                                               1e-5, op0=OP.mult, op1=OP.add))
        t_rc = ST("vector", lambda e: e.reciprocal(stat[:, 4:5], stat[:, 3:4]))
        t_rs2 = ST("scalar", lambda e: e.activation(stat[:, 5:6], stat[:, 4:5], AF.Sqrt),
                   waits=deps(t_rc))
        t_xn2 = ST("vector", lambda e: e.tensor_scalar(ln2n[:, :], lt2[:, :],
                                                       stat[:, 5:6], None, op0=OP.mult),
                   waits=deps(t_rs2, t_tp_prev))
        for dc in range(8):
            bnk = pB2p[dc % 2]
            t_tp = ST("tensor", lambda e, dc=dc, bnk=bnk: e.transpose(
                bnk[:, 0:128], ln2n[:, dc * 128:(dc + 1) * 128], identb[:, :]),
                waits=deps(t_xn2, t_te2, t_tev))
            t_tp_prev = t_tp
            t_te2 = t_te
            t_te = ST("vector", lambda e, dc=dc, t=t, bnk=bnk: e.tensor_copy(
                xn2T[:, dc * Tc + t * 128:dc * Tc + t * 128 + 128],
                bnk[:, 0:128]), waits=deps(t_tp))
        t_l2 = deps(t_te, t_te2)

    b.phase = "ph7"
    b.free(cm_l4, cm_l3, cm_l2, cm_l1)
    _, wqb = b.sb("wqb", [128, 1024], BF16)
    _, hT = b.sb("hT", [128, 32 * 512], BF16)
    _, fT = b.sb("fT", [128, 8 * 512], BF16)
    _, outt = b.sb("outt", [128, 1024])
    t_h = {}
    t_fe = {}
    t_od = {}
    t_oe = {}
    t_oe2 = {}
    for q in range(NQ):
        for fco in range(32):
            t_wl = DMA("sync", lambda e, fco=fco: e.dma_start(
                out=wqb[:, :], in_=bass.AP(wq1t_d.tensor, fco * 128,
                                           [[DF, 128], [128 * DF, 8], [1, 128]])),
                waits=deps(t_qdma, t_h, t_l2))
            bank = pA[fco % 2]
            t_fm = {}
            for dc in range(8):
                w0 = deps(t_wl, t_l2, t_h, t_prev, t_fe) if dc == 0 else ()
                t_fm = ST("tensor", lambda e, dc=dc, q=q, bank=bank: e.matmul(
                    bank[:, :], wqb[:, dc * 128:(dc + 1) * 128],
                    xn2T[:, dc * Tc + q * 512:dc * Tc + q * 512 + 512],
                    start=(dc == 0), stop=(dc == 7)), waits=w0)
            t_h = ST("vector", lambda e, fco=fco, bank=bank: e.tensor_scalar(
                hT[:, fco * 512:(fco + 1) * 512], bank[:, :],
                consts[:, C_B1 + fco:C_B1 + fco + 1], 0.0, op0=OP.add, op1=OP.max),
                waits=deps(t_fm))
        for half in range(2):
            t_f2 = {}
            for fc in range(32):
                t_w2l = DMA("sync", lambda e, fc=fc: e.dma_start(
                    out=wqb[:, :], in_=bass.AP(wq2t_d.tensor, fc * 128 * D,
                                               [[D, 128], [1, 1024]])),
                    waits=deps(t_h, t_fe, TK("tensor")))
                for do4 in range(4):
                    do = half * 4 + do4
                    w0 = deps(t_w2l, t_h, t_fe) if do4 == 0 else ()
                    t_f2 = ST("tensor", lambda e, do=do, do4=do4, fc=fc: e.matmul(
                        pA[do4][:, :], wqb[:, do * 128:(do + 1) * 128],
                        hT[:, fc * 512:(fc + 1) * 512], start=(fc == 0), stop=(fc == 31)),
                        waits=w0)
            for do4 in range(4):
                do = half * 4 + do4
                t_fe = ST("vector", lambda e, do=do, do4=do4: e.tensor_scalar(
                    fT[:, do * 512:(do + 1) * 512], pA[do4][:, :],
                    consts[:, C_B2 + do:C_B2 + do + 1], None, op0=OP.add),
                    waits=deps(t_f2))
        for ti in range(4):
            t = q * 4 + ti
            t_xr = DMA("sync", lambda e, t=t: e.dma_start(
                out=x2t[:, :], in_=bass.AP(x2_d.tensor, t * 128 * D,
                                           [[D, 128], [1, D]])), waits=deps(t_od))
            for do in range(8):
                bnk = pB2p[do % 2]
                t_tp = ST("tensor", lambda e, do=do, ti=ti, bnk=bnk: e.transpose(
                    bnk[:, 0:128],
                    fT[:, do * 512 + ti * 128:do * 512 + ti * 128 + 128], identb[:, :]),
                    waits=deps(t_fe, t_oe2))
                t_oe2 = t_oe
                t_oe = ST("vector", lambda e, do=do, bnk=bnk: e.scalar_tensor_tensor(
                    outt[:, do * 128:(do + 1) * 128], bnk[:, 0:128], 1.0,
                    x2t[:, do * 128:(do + 1) * 128],
                    op0=OP.mult, op1=OP.add), waits=deps(t_tp, t_xr))
            t_od = DMA("sync", lambda e, t=t: e.dma_start(
                out=bass.AP(out_d.tensor, t * 128 * D, [[D, 128], [1, D]]),
                in_=outt[:, :]), sem="oq", waits=deps(t_oe, t_oe2))

    if dbg:
        dbg_xh = nc.dram_tensor("dbg_xh", [Tc + 1, DF], BF16, kind="ExternalOutput").ap()
        dbg_yT = nc.dram_tensor("dbg_yT", [DI, Tc], BF16, kind="ExternalOutput").ap()
        dbg_x2 = nc.dram_tensor("dbg_x2", [Tc, D], BF16, kind="ExternalOutput").ap()
        dbg_S = nc.dram_tensor("dbg_S", [NS, H * PD], F32, kind="ExternalOutput").ap()
        DMA("sync", lambda e: e.dma_start(out=dbg_xh, in_=xh_d), sem="oq", waits=deps(t_od))
        DMA("sync", lambda e: e.dma_start(out=dbg_yT, in_=yT_d), sem="oq", waits=deps(t_od))
        DMA("sync", lambda e: e.dma_start(out=dbg_x2, in_=x2_d), sem="oq", waits=deps(t_od))
        t_od = DMA("sync", lambda e: e.dma_start(out=dbg_S, in_=cc2i), sem="oq", waits=deps(t_od))
    ST("scalar", lambda e: e.nop(), waits=deps(t_od, TK("oq"), TK("dq"), TK("gq")))

    b.emit()
    return nc


def _host_inputs(x, W_in, W_out, W1, W2, b1, b2, A_log, dt_bias, Bb, Cb, NT):
    Tc = L * NT
    ident = np.eye(128)
    m_ls = np.tril(np.ones((128, 128), np.float32)).T * SCALE
    maskS_np = np.tile(m_ls, (1, 4)).astype(ml_dtypes.bfloat16)
    base = dict(
        W_in=W_in, W_out=W_out, W1=W1, W2=W2,
        b1c=np.ascontiguousarray(b1.reshape(32, 128).T.astype(np.float32)),
        b2c=np.ascontiguousarray(b2.reshape(8, 128).T.astype(np.float32)),
        dtb_rep=np.broadcast_to(dt_bias, (128, H)).copy().astype(np.float32),
        alog_rep=np.broadcast_to(A_log, (128, H)).copy().astype(np.float32),
        bbias_rep=np.broadcast_to(np.tile(Bb[0], 2), (128, 2 * NS)).copy().astype(np.float32),
        cbias_rep=np.broadcast_to(Cb[0], (128, NS)).copy().astype(np.float32),
        ident_bf=ident.astype(ml_dtypes.bfloat16),
        ident_f32=ident.astype(np.float32),
        ones_col=np.ones((128, 1), np.float32),
        onesb=np.ones((128, 128), ml_dtypes.bfloat16),
        maskS=maskS_np,
    )
    maps = []
    T = x.shape[0]
    n = T // Tc
    for k in range(n):
        xs = np.zeros((Tc + 1, D), np.float32)
        t0 = k * Tc
        if t0 > 0:
            xs[0] = x[t0 - 1]
        xs[1:] = x[t0:t0 + Tc]
        mP = (np.arange(8) < k).astype(np.float32).reshape(8, 1)
        mIT = np.zeros((8, 8), np.float32)
        for i in range(8):
            for j in range(8):
                mIT[i, j] = 1.0 if (j < i < k) else 0.0
        maps.append(dict(base, x_sl=xs, maskP=mP, maskIT=mIT))
    return maps


def kernel(L0_attn_x, L0_attn_W_in, L0_attn_A_log, L0_attn_dt_bias,
           L0_attn_B_norm_bias, L0_attn_C_norm_bias, L0_attn_W_out,
           L0_ffn_W1, L0_ffn_b1, L0_ffn_W2, L0_ffn_b2):
    x = np.ascontiguousarray(np.asarray(L0_attn_x, np.float32)[0])
    T = x.shape[0]
    NT = T // (N_CORES * L)
    nc = build_program(NT)
    maps = _host_inputs(x, np.ascontiguousarray(L0_attn_W_in, dtype=np.float32),
                        np.ascontiguousarray(L0_attn_W_out, dtype=np.float32),
                        np.ascontiguousarray(L0_ffn_W1, dtype=np.float32),
                        np.ascontiguousarray(L0_ffn_W2, dtype=np.float32),
                        np.asarray(L0_ffn_b1, np.float32),
                        np.asarray(L0_ffn_b2, np.float32),
                        np.asarray(L0_attn_A_log, np.float32),
                        np.asarray(L0_attn_dt_bias, np.float32),
                        np.asarray(L0_attn_B_norm_bias, np.float32),
                        np.asarray(L0_attn_C_norm_bias, np.float32), NT)
    res = run_bass_kernel_spmd(nc, maps, core_ids=list(range(N_CORES)))
    out = np.concatenate([r["out"] for r in res.results], axis=0)
    return out.reshape(1, T, D).astype(np.float32)
